# version 04
# baseline (speedup 1.0000x reference)
import bisect
import numpy as np
from contextlib import ExitStack

import concourse.bass as bass
import concourse.mybir as mybir
from concourse.bass_utils import run_bass_kernel_spmd

F32 = mybir.dt.float32
BF16 = mybir.dt.bfloat16
I32 = mybir.dt.int32
AF = mybir.ActivationFunctionType
ALU = mybir.AluOpType
AX = mybir.AxisListType

ENGS = ("sync", "tensor", "scalar", "vector", "gpsimd")
NEG = -1.0e10
MBIG = 1.0e30
C = 1024
IC = 8
HID = 4096
NE = 16
TS = 512


class Prog:

    def __init__(self, sem_alloc):
        self.q = {e: [] for e in ENGS}
        self.marks = {e: [] for e in ENGS}
        self.inc_at = {e: set() for e in ENGS}
        self.last_w = {}
        self.readers = {}
        self.drain_pos = {e: -1 for e in ENGS}
        self.waited = {e: {} for e in ENGS}
        self.sem_alloc = sem_alloc
        self.dma_sems = {}
        self.n_dma = 0

    def _dep(self, eng, tok, want):
        if tok is None:
            return
        if tok[0] == "eng":
            _, e2, p = tok
            if e2 == eng:
                if eng != "tensor":
                    want["drain"] = max(want.get("drain", -1), p)
            else:
                k = ("auto", e2)
                want[k] = max(want.get(k, -1), p)
        else:
            _, dest, cnt = tok
            k = ("dma", dest)
            want[k] = max(want.get(k, -1), self.dma_sems[dest][1])

    def emit(self, eng, fn, reads=(), writes=(), dma=False):
        q = self.q[eng]
        want = {}
        for h in reads:
            self._dep(eng, self.last_w.get(h.name), want)
        for h in writes:
            lw = self.last_w.get(h.name)
            if not (dma and lw is not None and lw[0] == "dma" and lw[1] == h.name):
                self._dep(eng, lw, want)
            for tok in self.readers.get(h.name, ()):
                self._dep(eng, tok, want)
        self.seq = getattr(self, "seq", 0) + 1
        sq = self.seq
        dp = want.pop("drain", -1)
        if dp > self.drain_pos[eng]:
            q.append(("drain", sq))
            self.drain_pos[eng] = len(q) - 1
        for key in sorted(want, key=str):
            p = want[key]
            if key[0] == "auto":
                e2 = key[1]
                ms = self.marks[e2]
                i = bisect.bisect_left(ms, p)
                if i == len(ms):
                    ms.append(p)
                    self.inc_at[e2].add(p)
                    mp = p
                else:
                    mp = ms[i]
                if self.waited[eng].get(key, -1) < mp:
                    q.append(("wait_auto", e2, mp, sq))
                    self.waited[eng][key] = mp
            else:
                dest = key[1]
                if self.waited[eng].get(key, -1) < p:
                    q.append(("wait_dma", dest, p, sq))
                    self.waited[eng][key] = p
        pos = len(q)
        if dma:
            assert len(writes) == 1
            dest = writes[0].name
            if dest not in self.dma_sems:
                self.dma_sems[dest] = [self.sem_alloc(f"sdma_{len(self.dma_sems)}"), 0]
            self.dma_sems[dest][1] += 1
            self.n_dma += 1
            tok = ("dma", dest, self.dma_sems[dest][1])
            q.append(("op_dma", fn, dest, sq))
        else:
            tok = ("eng", eng, pos)
            q.append(("op", fn, pos, sq))
        for h in writes:
            self.last_w[h.name] = tok
            self.readers[h.name] = []
        for h in reads:
            self.readers.setdefault(h.name, []).append(tok)
        return tok

    def barrier_dma(self, eng, dest_name):
        if dest_name in self.dma_sems:
            self.q[eng].append(("wait_dma", dest_name, self.dma_sems[dest_name][1],
                                getattr(self, "seq", 0) + 10**9))

    def play(self, block, auto_sems, trunc=None):
        if trunc is not None:
            newq = {}
            dma_tot = {}
            for e in ENGS:
                kept = []
                for ent in self.q[e]:
                    sq = ent[-1] if ent[0] != "drain" else ent[1]
                    if sq > trunc:
                        continue
                    kept.append(ent)
                    if ent[0] == "op_dma":
                        dma_tot[ent[2]] = dma_tot.get(ent[2], 0) + 1
                newq[e] = kept
            self.q = newq
            for dest, tot in dma_tot.items():
                self.q["gpsimd"].append(("wait_dma", dest, tot, 0))
        rank = {e: {p: i + 1 for i, p in enumerate(self.marks[e])} for e in ENGS}

        def mk_body(ename):
            entries = self.q[ename]
            inc_at = self.inc_at[ename]

            def body(e):
                for ent in entries:
                    k = ent[0]
                    if k == "drain":
                        e.drain()
                    elif k == "wait_auto":
                        e.wait_ge(auto_sems[ent[1]], rank[ent[1]][ent[2]])
                    elif k == "wait_dma":
                        e.wait_ge(self.dma_sems[ent[1]][0], 16 * ent[2])
                    elif k == "op":
                        ins = ent[1](e)
                        if ent[2] in inc_at:
                            ins.then_inc(auto_sems[ename], 1)
                    else:
                        ent[1](e).then_inc(self.dma_sems[ent[2]][0], 16)
            return body

        block.sync(mk_body("sync"))
        block.tensor(mk_body("tensor"))
        block.scalar(mk_body("scalar"))
        block.vector(mk_body("vector"))
        block.gpsimd(mk_body("gpsimd"))


def build(T_seq=4096, p1_stages=None, do_p2=True, do_p3=True, p3_mode='full'):
    NST = T_seq // TS
    NQ = T_seq // 256
    OWN = T_seq // 2
    NST3 = OWN // TS
    NKT = T_seq // 128

    nc = bass.Bass("TRN2", target_bir_lowering=False,
                   dynamic_dma_scratch_size=1024)

    d_xt = nc.dram_tensor("x_t", [C, T_seq], F32, kind="ExternalInput")
    d_wa = nc.dram_tensor("w_attn", [C, 3 * C], F32, kind="ExternalInput")
    d_ba = nc.dram_tensor("ba_r", [128, 24], F32, kind="ExternalInput")
    d_bav = nc.dram_tensor("bav_row", [1, C], F32, kind="ExternalInput")
    d_wp = nc.dram_tensor("w_proj", [C, C], F32, kind="ExternalInput")
    d_bp = nc.dram_tensor("bp_r", [128, 8], F32, kind="ExternalInput")
    d_rwt = nc.dram_tensor("rw_t", [C, NE], F32, kind="ExternalInput")
    d_rb = nc.dram_tensor("rb_r", [NE, 1], F32, kind="ExternalInput")
    d_w1t = nc.dram_tensor("w1_t", [C, HID], F32, kind="ExternalInput")
    d_b1 = nc.dram_tensor("b1_r", [128, 32], F32, kind="ExternalInput")
    d_w2t = nc.dram_tensor("w2_t", [HID, C], F32, kind="ExternalInput")
    d_b2 = nc.dram_tensor("b2_r", [128, 8], F32, kind="ExternalInput")
    d_sel = nc.dram_tensor("sel", [1, 2], F32, kind="ExternalInput")
    d_out = nc.dram_tensor("out_t", [C, OWN], F32, kind="ExternalOutput")

    d_ksp = nc.dram_tensor("k_sp", [C, 2 * T_seq], BF16, kind="Internal")
    d_vsp = nc.dram_tensor("v_sp", [T_seq, 2 * C], BF16, kind="Internal")
    d_qsp = nc.dram_tensor("q_sp", [C, 2 * T_seq], BF16, kind="Internal")
    d_w1s = nc.dram_tensor("w1_s", [C, 2 * HID], BF16, kind="Internal")
    d_w2s = nc.dram_tensor("w2_s", [HID, 2 * C], BF16, kind="Internal")
    d_was = nc.dram_tensor("wa_s", [C, 6 * C], BF16, kind="Internal")
    d_wps = nc.dram_tensor("wp_s", [C, 2 * C], BF16, kind="Internal")
    d_xmsp = nc.dram_tensor("xm_sp", [C, T_seq], F32, kind="Internal")
    d_x2sp = nc.dram_tensor("x2_sp", [C, OWN], F32, kind="Internal")

    a_xt = d_xt.ap(); a_wa = d_wa.ap(); a_wp = d_wp.ap(); a_rwt = d_rwt.ap()
    a_w1t = d_w1t.ap(); a_w2t = d_w2t.ap()
    a_ksp = d_ksp.ap(); a_vsp = d_vsp.ap(); a_qsp = d_qsp.ap()
    a_xmsp = d_xmsp.ap(); a_x2sp = d_x2sp.ap(); a_out = d_out.ap()
    a_w1s = d_w1s.ap(); a_w2s = d_w2s.ap(); a_was = d_was.ap(); a_wps = d_wps.ap()


    with ExitStack() as ctx:
        E = ctx.enter_context
        xs = [E(nc.sbuf_tensor(f"xs{i}", [128, TS], F32)) for i in range(IC)]
        xlh = [E(nc.sbuf_tensor(f"xlh{i}", [128, TS], BF16)) for i in range(IC)]
        xll = [E(nc.sbuf_tensor(f"xll{i}", [128, TS], BF16)) for i in range(IC)]
        ys = [E(nc.sbuf_tensor(f"ys{i}", [128, TS], F32)) for i in range(IC)]
        wA = [E(nc.sbuf_tensor(f"wA{i}", [128, 2048], BF16)) for i in range(16)]
        wB = [E(nc.sbuf_tensor(f"wB{i}", [128, 2048], BF16)) for i in range(16)]
        hsb = [E(nc.sbuf_tensor(f"hsb{i}", [128, 256], F32)) for i in range(2)]
        hh = [E(nc.sbuf_tensor(f"hh{i}", [128, 256], BF16)) for i in range(2)]
        hl = [E(nc.sbuf_tensor(f"hl{i}", [128, 256], BF16)) for i in range(2)]
        kvq = [E(nc.sbuf_tensor(f"kvq{i}", [128, 2 * TS], BF16)) for i in range(3)]
        pth = [E(nc.sbuf_tensor(f"pth{i}", [128, 128], BF16)) for i in range(2)]
        ptl = [E(nc.sbuf_tensor(f"ptl{i}", [128, 128], BF16)) for i in range(2)]
        rwt_sb = [E(nc.sbuf_tensor(f"rwt{i}", [128, NE], F32)) for i in range(IC)]
        rwh = [E(nc.sbuf_tensor(f"rwh{i}", [128, NE], BF16)) for i in range(IC)]
        rwl = [E(nc.sbuf_tensor(f"rwl{i}", [128, NE], BF16)) for i in range(IC)]
        rowbuf = E(nc.sbuf_tensor("rowbuf", [1, 2048], F32))
        io = E(nc.sbuf_tensor("io", [128, 128], F32))
        esel = E(nc.sbuf_tensor("esel", [16, 2048], BF16))
        ident = E(nc.sbuf_tensor("ident", [128, 128], F32))
        trib = E(nc.sbuf_tensor("trib", [128, 128], F32))
        fullm = E(nc.sbuf_tensor("fullm", [128, 128], F32))
        ones_c = E(nc.sbuf_tensor("ones_c", [128, 1], F32))
        ones_r = E(nc.sbuf_tensor("ones_r", [1, 128], F32))
        selb = E(nc.sbuf_tensor("selb", [128, 2], F32))
        sel_sb = E(nc.sbuf_tensor("sel_sb", [1, 2], F32))
        lg_sb = E(nc.sbuf_tensor("lg_sb", [NE, TS], F32))
        ltm = E(nc.sbuf_tensor("ltm", [128, 64], F32))
        top8 = E(nc.sbuf_tensor("top8", [128, 32], F32))
        mtm = E(nc.sbuf_tensor("mtm", [128, 64], F32))
        mfm = E(nc.sbuf_tensor("mfm", [NE, TS], F32))
        mmfm = E(nc.sbuf_tensor("mmfm", [NE, TS], BF16))
        ba_sb = E(nc.sbuf_tensor("ba_sb", [128, 24], F32))
        baq_sb = E(nc.sbuf_tensor("baq_sb", [128, 8], F32))
        bav_sb = E(nc.sbuf_tensor("bav_sb", [1, C], F32))
        bp_sb = E(nc.sbuf_tensor("bp_sb", [128, 8], F32))
        b1_sb = E(nc.sbuf_tensor("b1_sb", [128, 32], F32))
        b2_sb = E(nc.sbuf_tensor("b2_sb", [128, 8], F32))
        rb_sb = E(nc.sbuf_tensor("rb_sb", [NE, 1], F32))
        rm_sb = E(nc.sbuf_tensor("rm_sb", [128, 8], F32))
        sm_sb = E(nc.sbuf_tensor("sm_sb", [128, 8], F32))
        nm_sb = E(nc.sbuf_tensor("nm_sb", [128, 1], F32))
        ls_sb = E(nc.sbuf_tensor("ls_sb", [128, 1], F32))
        rl_sb = E(nc.sbuf_tensor("rl_sb", [128, 1], F32))
        eps_sb = E(nc.sbuf_tensor("eps_sb", [1, 1], F32))

        PB = [E(nc.psum_tensor(f"pb{i}", [128, 512], F32)) for i in range(8)]

        sem_auto = {e: E(nc.semaphore(f"sa_{e}")) for e in ENGS}
        P = Prog(lambda name: E(nc.semaphore(name)))

        def mm(out_ap, l_ap, r_ap, start, stop):
            return lambda e: e.matmul(out_ap, l_ap, r_ap, start=start, stop=stop)

        def dma(out_ap, in_ap):
            return lambda e: e.dma_start(out=out_ap, in_=in_ap)

        P.emit("sync", dma(sel_sb[:, :], d_sel.ap()), reads=[d_sel], writes=[sel_sb], dma=True)
        P.emit("sync", dma(ba_sb[:, :], d_ba.ap()), reads=[d_ba], writes=[ba_sb], dma=True)
        P.emit("sync", dma(bav_sb[:, :], d_bav.ap()), reads=[d_bav], writes=[bav_sb], dma=True)
        P.emit("sync", dma(bp_sb[:, :], d_bp.ap()), reads=[d_bp], writes=[bp_sb], dma=True)
        P.emit("sync", dma(b1_sb[:, :], d_b1.ap()), reads=[d_b1], writes=[b1_sb], dma=True)
        P.emit("sync", dma(b2_sb[:, :], d_b2.ap()), reads=[d_b2], writes=[b2_sb], dma=True)
        P.emit("sync", dma(rb_sb[:, :], d_rb.ap()), reads=[d_rb], writes=[rb_sb], dma=True)
        for i in range(IC):
            P.emit("sync", dma(rwt_sb[i][:, :], a_rwt[128 * i:128 * i + 128, :]),
                   reads=[d_rwt], writes=[rwt_sb[i]], dma=True)

        P.emit("gpsimd", lambda e: e.iota(io[:, :], pattern=[[1, 128]], base=0,
                                          channel_multiplier=-1,
                                          allow_small_or_imprecise_dtypes=True),
               writes=[io])
        for q4 in range(4):
            P.emit("gpsimd", (lambda q4: (lambda e: e.iota(
                xs[0][0:16, 0:512], pattern=[[1, 4], [0, 128]], base=4 * q4,
                channel_multiplier=-1, allow_small_or_imprecise_dtypes=True)))(q4),
                   writes=[xs[0]])
            P.emit("vector", (lambda q4: (lambda e: e.tensor_scalar(
                esel[:, 512 * q4:512 * q4 + 512], xs[0][0:16, 0:512],
                0.0, None, ALU.is_equal)))(q4),
                   reads=[xs[0]], writes=[esel])
        P.emit("gpsimd", lambda e: e.memset(ones_c[:, :], 1.0), writes=[ones_c])
        P.emit("gpsimd", lambda e: e.memset(ones_r[:, :], 1.0), writes=[ones_r])
        P.emit("vector", lambda e: e.memset(fullm[:, :], NEG), writes=[fullm])
        P.emit("vector", lambda e: e.memset(eps_sb[:, :], 1e-5), writes=[eps_sb])
        P.emit("vector", lambda e: e.tensor_scalar(ident[:, :], io[:, :], 0.0, None, ALU.is_equal),
               reads=[io], writes=[ident])
        P.emit("vector", lambda e: e.tensor_scalar(trib[:, :], io[:, :], 0.0, NEG, ALU.is_gt, ALU.mult),
               reads=[io], writes=[trib])
        P.emit("tensor", mm(PB[0][:, 0:2], ones_r[:, :], sel_sb[:, :], True, True),
               reads=[ones_r, sel_sb], writes=[PB[0]])
        P.emit("scalar", lambda e: e.copy(selb[:, :], PB[0][:, 0:2]), reads=[PB[0]], writes=[selb])
        P.emit("vector", lambda e: e.tensor_scalar(io[:, :], trib[:, :], selb[:, 1:2], None, ALU.mult),
               reads=[trib, selb], writes=[io])
        P.emit("vector", lambda e: e.tensor_scalar(fullm[:, :], fullm[:, :], selb[:, 0:1], None, ALU.mult),
               reads=[fullm, selb], writes=[fullm])
        P.emit("vector", lambda e: e.tensor_add(fullm[:, :], fullm[:, :], io[:, :]),
               reads=[fullm, io], writes=[fullm])
        P.emit("vector", lambda e: e.tensor_scalar(trib[:, :], trib[:, :], selb[:, 0:1], None, ALU.mult),
               reads=[trib, selb], writes=[trib])
        maskA, maskB = trib, fullm
        P.emit("vector", lambda e: e.tensor_scalar(baq_sb[:, :], ba_sb[:, 0:8], 1.0 / 32.0, None, ALU.mult),
               reads=[ba_sb], writes=[baq_sb])

        _pc = [0]

        def emit_split(src_ap, src_h, hi_ap_fn, lo_ap_fn, rows, cols):
            for r in range(rows // 128):
                for cc in range(cols // 512):
                    st8 = xs[_pc[0] % 4]
                    kb8 = kvq[_pc[0] % 3]
                    _pc[0] += 1
                    P.emit("sync", dma(st8[:, 0:512],
                                       src_ap[128 * r:128 * r + 128, 512 * cc:512 * cc + 512]),
                           reads=[src_h], writes=[st8], dma=True)
                    P.emit("vector", (lambda a, b: (lambda e: e.tensor_copy(a, b)))(
                        kb8[:, 0:512], st8[:, 0:512]),
                           reads=[st8], writes=[kb8])
                    P.emit("vector", (lambda a, b, c: (lambda e: e.tensor_sub(a, b, c)))(
                        kb8[:, 512:1024], st8[:, 0:512], kb8[:, 0:512]),
                           reads=[st8, kb8], writes=[kb8])
                    P.emit("scalar", dma(hi_ap_fn(r, cc), kb8[:, 0:512]),
                           reads=[kb8], writes=[hi_ap_fn.dest], dma=True)
                    P.emit("scalar", dma(lo_ap_fn(r, cc), kb8[:, 512:1024]),
                           reads=[kb8], writes=[lo_ap_fn.dest], dma=True)

        def _mk(dest_h, base_ap, lo_off):
            def hi(r, cc):
                return base_ap[128 * r:128 * r + 128, 512 * cc:512 * cc + 512]
            def lo(r, cc):
                return base_ap[128 * r:128 * r + 128, lo_off + 512 * cc:lo_off + 512 * cc + 512]
            hi.dest = dest_h
            lo.dest = dest_h
            return hi, lo

        h1, l1 = _mk(d_w1s, a_w1s, HID)
        emit_split(a_w1t, d_w1t, h1, l1, C, HID)
        h2, l2 = _mk(d_w2s, a_w2s, C)
        emit_split(a_w2t, d_w2t, h2, l2, HID, C)
        h3, l3 = _mk(d_was, a_was, 3 * C)
        emit_split(a_wa, d_wa, h3, l3, C, 3 * C)
        h4, l4 = _mk(d_wps, a_wps, C)
        emit_split(a_wp, d_wp, h4, l4, C, C)
        for i in range(IC):
            P.emit("vector", (lambda i: (lambda e: e.tensor_copy(rwh[i][:, :], rwt_sb[i][:, :])))(i),
                   reads=[rwt_sb[i]], writes=[rwh[i]])
            P.emit("vector", (lambda i: (lambda e: e.tensor_sub(rwl[i][:, :], rwt_sb[i][:, :], rwh[i][:, :])))(i),
                   reads=[rwt_sb[i], rwh[i]], writes=[rwl[i]])

        RB_MU, RB_E2, RB_T, RB_STD, RB_RSTD, RB_MRS = (
            (0, 512), (512, 512), (1024, 512), (1024, 512), (1536, 512), (0, 512))

        def rb(which):
            o, w = which
            return rowbuf[0:1, o:o + TS]

        def emit_ln(src, dsth, dstl):
            for i in range(IC):
                P.emit("tensor", mm(PB[6][0:1, 0:TS], ones_c[:, :], src[i][:, 0:TS],
                                    i == 0, i == IC - 1),
                       reads=[ones_c, src[i]], writes=[PB[6]])
                P.emit("scalar", (lambda a, b: (lambda e: e.square(a, b)))(
                    ys[i % 2][:, 0:TS], src[i][:, 0:TS]),
                       reads=[src[i]], writes=[ys[i % 2]])
                P.emit("tensor", mm(PB[7][0:1, 0:TS], ones_c[:, :], ys[i % 2][:, 0:TS],
                                    i == 0, i == IC - 1),
                       reads=[ones_c, ys[i % 2]], writes=[PB[7]])
            P.emit("vector", lambda e: e.tensor_scalar(rb(RB_MU), PB[6][0:1, 0:TS], 1.0 / C, None, ALU.mult),
                   reads=[PB[6]], writes=[rowbuf])
            P.emit("vector", lambda e: e.tensor_scalar(rb(RB_E2), PB[7][0:1, 0:TS], 1.0 / C, None, ALU.mult),
                   reads=[PB[7]], writes=[rowbuf])
            P.emit("vector", lambda e: e.tensor_mul(rb(RB_T), rb(RB_MU), rb(RB_MU)),
                   reads=[rowbuf], writes=[rowbuf])
            P.emit("vector", lambda e: e.tensor_sub(rb(RB_E2), rb(RB_E2), rb(RB_T)),
                   reads=[rowbuf], writes=[rowbuf])
            P.emit("scalar", lambda e: e.activation(rb(RB_STD), rb(RB_E2), AF.Sqrt,
                                                    bias=eps_sb[0:1, 0:1]),
                   reads=[rowbuf, eps_sb], writes=[rowbuf])
            P.emit("vector", lambda e: e.reciprocal(rb(RB_RSTD), rb(RB_STD)),
                   reads=[rowbuf], writes=[rowbuf])
            P.emit("vector", lambda e: e.tensor_mul(rb(RB_MRS), rb(RB_MU), rb(RB_RSTD)),
                   reads=[rowbuf], writes=[rowbuf])
            P.emit("tensor", mm(PB[6][:, 0:TS], ones_r[:, :], rb(RB_RSTD), True, True),
                   reads=[ones_r, rowbuf], writes=[PB[6]])
            P.emit("tensor", mm(PB[7][:, 0:TS], ones_r[:, :], rb(RB_MRS), True, True),
                   reads=[ones_r, rowbuf], writes=[PB[7]])
            for i in range(IC):
                P.emit("vector", (lambda a, b: (lambda e: e.tensor_mul(a, b, PB[6][:, 0:TS])))(
                    ys[i][:, 0:TS], src[i][:, 0:TS]),
                       reads=[src[i], PB[6]], writes=[ys[i]])
            for i in range(IC):
                P.emit("vector", (lambda a: (lambda e: e.tensor_sub(a, a, PB[7][:, 0:TS])))(
                    ys[i][:, 0:TS]),
                       reads=[ys[i], PB[7]], writes=[ys[i]])
            for i in range(IC):
                P.emit("vector", (lambda a, b: (lambda e: e.tensor_copy(a, b)))(
                    dsth[i][:, 0:TS], ys[i][:, 0:TS]),
                       reads=[ys[i]], writes=[dsth[i]])
            for i in range(IC):
                P.emit("vector", (lambda a, b, c: (lambda e: e.tensor_sub(a, b, c)))(
                    dstl[i][:, 0:TS], ys[i][:, 0:TS], dsth[i][:, 0:TS]),
                       reads=[ys[i], dsth[i]], writes=[dstl[i]])

        def emit_router(srch, srcl):
            for i in range(IC):
                P.emit("tensor", mm(PB[5][0:NE, 0:TS], rwh[i][:, :], srch[i][:, 0:TS],
                                    i == 0, False),
                       reads=[rwh[i], srch[i]], writes=[PB[5]])
                P.emit("tensor", mm(PB[5][0:NE, 0:TS], rwh[i][:, :], srcl[i][:, 0:TS],
                                    False, False),
                       reads=[rwh[i], srcl[i]], writes=[PB[5]])
                P.emit("tensor", mm(PB[5][0:NE, 0:TS], rwl[i][:, :], srch[i][:, 0:TS],
                                    False, i == IC - 1),
                       reads=[rwl[i], srch[i]], writes=[PB[5]])
            P.emit("scalar", lambda e: e.activation(lg_sb[0:NE, 0:TS], PB[5][0:NE, 0:TS],
                                                    AF.Identity, bias=rb_sb[0:NE, 0:1]),
                   reads=[PB[5], rb_sb], writes=[lg_sb])
            nt = TS // 128
            for t in range(nt):
                P.emit("tensor", (lambda t: (lambda e: e.transpose(
                    PB[6][:, 16 * t:16 * t + 16], lg_sb[0:NE, 128 * t:128 * t + 128],
                    ident[0:NE, 0:NE])))(t),
                       reads=[lg_sb, ident], writes=[PB[6]])
            P.emit("scalar", lambda e: e.copy(ltm[:, 0:16 * nt], PB[6][:, 0:16 * nt]),
                   reads=[PB[6]], writes=[ltm])
            for t in range(nt):
                P.emit("vector", (lambda t: (lambda e: e.max(
                    out=top8[:, 8 * t:8 * t + 8], in_=ltm[:, 16 * t:16 * t + 16])))(t),
                       reads=[ltm], writes=[top8])
            for t in range(nt):
                P.emit("vector", (lambda t: (lambda e: e.tensor_scalar(
                    mtm[:, 16 * t:16 * t + 16], ltm[:, 16 * t:16 * t + 16],
                    top8[:, 8 * t + 3:8 * t + 4], None, ALU.is_ge)))(t),
                       reads=[ltm, top8], writes=[mtm])
            for t in range(nt):
                P.emit("tensor", (lambda t: (lambda e: e.transpose(
                    PB[7][0:NE, 128 * t:128 * t + 128], mtm[:, 16 * t:16 * t + 16],
                    ident[:, :])))(t),
                       reads=[mtm, ident], writes=[PB[7]])
            P.emit("scalar", lambda e: e.copy(mfm[0:NE, 0:TS], PB[7][0:NE, 0:TS]),
                   reads=[PB[7]], writes=[mfm])
            P.emit("vector", lambda e: e.tensor_scalar(mmfm[0:NE, 0:TS], mfm[0:NE, 0:TS],
                                                       MBIG, -MBIG, ALU.mult, ALU.add),
                   reads=[mfm], writes=[mmfm])

        def emit_ffn(srch, srcl, dst):
            for q in range(4):
                buf = wA if q % 2 == 0 else wB
                for i in range(IC):
                    P.emit("sync", dma(buf[i][:, 0:1024],
                                       a_w1s[128 * i:128 * i + 128, 1024 * q:1024 * q + 1024]),
                           reads=[d_w1s], writes=[buf[i]], dma=True)
                    P.emit("sync", dma(buf[i][:, 1024:2048],
                                       a_w1s[128 * i:128 * i + 128,
                                             HID + 1024 * q:HID + 1024 * q + 1024]),
                           reads=[d_w1s], writes=[buf[i]], dma=True)
                for hcl in range(8):
                    rr = 1024 * q + 128 * hcl
                    P.emit("sync", dma(buf[8 + hcl][:, 0:1024], a_w2s[rr:rr + 128, 0:1024]),
                           reads=[d_w2s], writes=[buf[8 + hcl]], dma=True)
                    P.emit("sync", dma(buf[8 + hcl][:, 1024:2048], a_w2s[rr:rr + 128, 1024:2048]),
                           reads=[d_w2s], writes=[buf[8 + hcl]], dma=True)
                for g in range(TS // 256):
                    gsl = slice(256 * g, 256 * g + 256)
                    for hcl in range(8):
                        hc = 8 * q + hcl
                        hb = PB[4 + hcl % 2]
                        for i in range(IC):
                            w1h = buf[i][:, 128 * hcl:128 * hcl + 128]
                            w1l = buf[i][:, 1024 + 128 * hcl:1024 + 128 * hcl + 128]
                            P.emit("tensor", mm(hb[:, 0:256], w1h, srch[i][:, gsl],
                                                i == 0, False),
                                   reads=[buf[i], srch[i]], writes=[hb])
                            P.emit("tensor", mm(hb[:, 0:256], w1h, srcl[i][:, gsl],
                                                False, False),
                                   reads=[buf[i], srcl[i]], writes=[hb])
                            P.emit("tensor", mm(hb[:, 0:256], w1l, srch[i][:, gsl],
                                                False, False),
                                   reads=[buf[i], srch[i]], writes=[hb])
                        P.emit("tensor", mm(hb[:, 0:256], esel[:, 128 * (hc // 2):128 * (hc // 2) + 128],
                                            mmfm[0:NE, gsl], False, True),
                               reads=[esel, mmfm], writes=[hb])
                        hs = hsb[hcl % 2]
                        P.emit("scalar", (lambda a, b, hc: (lambda e: e.activation(
                            a, b, AF.Relu, bias=b1_sb[:, hc:hc + 1])))(
                                hs[:, 0:256], hb[:, 0:256], hc),
                               reads=[hb, b1_sb], writes=[hs])
                        hhc, hlc = hh[hcl % 2], hl[hcl % 2]
                        P.emit("vector", (lambda a, b: (lambda e: e.tensor_copy(a, b)))(
                            hhc[:, 0:256], hs[:, 0:256]),
                               reads=[hs], writes=[hhc])
                        P.emit("vector", (lambda a, b, c: (lambda e: e.tensor_sub(a, b, c)))(
                            hlc[:, 0:256], hs[:, 0:256], hhc[:, 0:256]),
                               reads=[hs, hhc], writes=[hlc])
                        for oc in range(8):
                            yb = PB[oc // 2]
                            ysl = slice(256 * (oc % 2), 256 * (oc % 2) + 256)
                            w2h = buf[8 + hcl][:, 128 * oc:128 * oc + 128]
                            w2l = buf[8 + hcl][:, 1024 + 128 * oc:1024 + 128 * oc + 128]
                            P.emit("tensor", mm(yb[:, ysl], w2h, hhc[:, 0:256],
                                                start=(hcl == 0 and oc % 2 == 0), stop=False),
                                   reads=[buf[8 + hcl], hhc], writes=[yb])
                            P.emit("tensor", mm(yb[:, ysl], w2h, hlc[:, 0:256],
                                                start=False, stop=False),
                                   reads=[buf[8 + hcl], hlc], writes=[yb])
                            P.emit("tensor", mm(yb[:, ysl], w2l, hhc[:, 0:256],
                                                start=False,
                                                stop=(hcl == 7 and oc % 2 == 1)),
                                   reads=[buf[8 + hcl], hhc], writes=[yb])
                    for oc in range(8):
                        yb = PB[oc // 2]
                        ysl = slice(256 * (oc % 2), 256 * (oc % 2) + 256)
                        if q == 0:
                            P.emit("scalar", (lambda a, b, oc: (lambda e: e.activation(
                                a, b, AF.Identity, bias=b2_sb[:, oc:oc + 1])))(
                                    dst[oc][:, gsl], yb[:, ysl], oc),
                                   reads=[yb, b2_sb], writes=[dst[oc]])
                        else:
                            P.emit("vector", (lambda a, b: (lambda e: e.tensor_add(a, a, b)))(
                                dst[oc][:, gsl], yb[:, ysl]),
                                   reads=[yb, dst[oc]], writes=[dst[oc]])

        for st in range(NST if p1_stages is None else p1_stages):
            csl = slice(TS * st, TS * st + TS)
            for i in range(IC):
                P.emit("sync", dma(xs[i][:, 0:TS], a_xt[128 * i:128 * i + 128, csl]),
                       reads=[d_xt], writes=[xs[i]], dma=True)
            emit_ln(xs, xlh, xll)
            emit_router(xlh, xll)
            emit_ffn(xlh, xll, ys)
            for oc in range(IC):
                P.emit("vector", (lambda a, b: (lambda e: e.scalar_tensor_tensor(
                    a, b, 0.5, a, ALU.mult, ALU.add)))(xs[oc][:, 0:TS], ys[oc][:, 0:TS]),
                       reads=[ys[oc], xs[oc]], writes=[xs[oc]])
                P.emit("scalar", dma(a_xmsp[128 * oc:128 * oc + 128, csl], xs[oc][:, 0:TS]),
                       reads=[xs[oc]], writes=[d_xmsp], dma=True)
            emit_ln(xs, xlh, xll)
            for i in range(IC):
                P.emit("sync", dma(wA[i][:, 0:1024], a_was[128 * i:128 * i + 128, C:2 * C]),
                       reads=[d_was], writes=[wA[i]], dma=True)
                P.emit("sync", dma(wA[i][:, 1024:2048],
                                   a_was[128 * i:128 * i + 128, 4 * C:5 * C]),
                       reads=[d_was], writes=[wA[i]], dma=True)
            for i in range(IC):
                P.emit("sync", dma(wA[8 + i][:, 0:1024], a_was[128 * i:128 * i + 128, 2 * C:3 * C]),
                       reads=[d_was], writes=[wA[8 + i]], dma=True)
                P.emit("sync", dma(wA[8 + i][:, 1024:2048],
                                   a_was[128 * i:128 * i + 128, 5 * C:6 * C]),
                       reads=[d_was], writes=[wA[8 + i]], dma=True)
            for i in range(IC):
                P.emit("sync", dma(wB[i][:, 0:1024], a_was[128 * i:128 * i + 128, 0:C]),
                       reads=[d_was], writes=[wB[i]], dma=True)
                P.emit("sync", dma(wB[i][:, 1024:2048],
                                   a_was[128 * i:128 * i + 128, 3 * C:4 * C]),
                       reads=[d_was], writes=[wB[i]], dma=True)
            for kc in range(8):
                pb = PB[kc % 2]
                for i in range(IC):
                    wh_ = wA[i][:, 128 * kc:128 * kc + 128]
                    wl_ = wA[i][:, 1024 + 128 * kc:1024 + 128 * kc + 128]
                    P.emit("tensor", mm(pb[:, 0:TS], wh_, xlh[i][:, 0:TS], i == 0, False),
                           reads=[wA[i], xlh[i]], writes=[pb])
                    P.emit("tensor", mm(pb[:, 0:TS], wh_, xll[i][:, 0:TS], False, False),
                           reads=[wA[i], xll[i]], writes=[pb])
                    P.emit("tensor", mm(pb[:, 0:TS], wl_, xlh[i][:, 0:TS],
                                        False, i == IC - 1),
                           reads=[wA[i], xlh[i]], writes=[pb])
                kb = kvq[kc % 3]
                P.emit("scalar", (lambda a, b, kc: (lambda e: e.activation(
                    a, b, AF.Identity, bias=ba_sb[:, 8 + kc:9 + kc])))(
                        kb[:, 0:TS], pb[:, 0:TS], kc),
                       reads=[pb, ba_sb], writes=[kb])
                P.emit("vector", (lambda a, b, c, kc: (lambda e: e.scalar_tensor_tensor(
                    a, b, ba_sb[:, 8 + kc:9 + kc], c, ALU.add, ALU.subtract)))(
                        kb[:, TS:2 * TS], pb[:, 0:TS], kb[:, 0:TS], kc),
                       reads=[pb, ba_sb, kb], writes=[kb])
                P.emit("scalar", dma(a_ksp[128 * kc:128 * kc + 128, csl], kb[:, 0:TS]),
                       reads=[kb], writes=[d_ksp], dma=True)
                P.emit("scalar", dma(a_ksp[128 * kc:128 * kc + 128,
                                           T_seq + TS * st:T_seq + TS * st + TS],
                                     kb[:, TS:2 * TS]),
                       reads=[kb], writes=[d_ksp], dma=True)
            for t in range(TS // 128):
                for hf in range(2):
                    pb = PB[2 + (2 * t + hf) % 2]
                    for i in range(IC):
                        xh_t = xlh[i][:, 128 * t:128 * t + 128]
                        xl_t = xll[i][:, 128 * t:128 * t + 128]
                        wvh = wA[8 + i][:, 512 * hf:512 * hf + 512]
                        wvl = wA[8 + i][:, 1024 + 512 * hf:1024 + 512 * hf + 512]
                        P.emit("tensor", mm(pb[:, 0:512], xh_t, wvh, i == 0, False),
                               reads=[xlh[i], wA[8 + i]], writes=[pb])
                        P.emit("tensor", mm(pb[:, 0:512], xl_t, wvh, False, False),
                               reads=[xll[i], wA[8 + i]], writes=[pb])
                        P.emit("tensor", mm(pb[:, 0:512], xh_t, wvl, False, False),
                               reads=[xlh[i], wA[8 + i]], writes=[pb])
                    P.emit("tensor", mm(pb[:, 0:512], ones_r[:, :],
                                        bav_sb[0:1, 512 * hf:512 * hf + 512], False, True),
                           reads=[ones_r, bav_sb], writes=[pb])
                    vb = kvq[(2 * t + hf) % 3]
                    P.emit("scalar", (lambda a, b: (lambda e: e.copy(a, b)))(
                        vb[:, 0:512], pb[:, 0:512]),
                           reads=[pb], writes=[vb])
                    P.emit("vector", (lambda a, b, c: (lambda e: e.tensor_sub(a, b, c)))(
                        vb[:, 512:1024], pb[:, 0:512], vb[:, 0:512]),
                           reads=[pb, vb], writes=[vb])
                    P.emit("scalar", dma(a_vsp[TS * st + 128 * t:TS * st + 128 * t + 128,
                                               512 * hf:512 * hf + 512], vb[:, 0:512]),
                           reads=[vb], writes=[d_vsp], dma=True)
                    P.emit("scalar", dma(a_vsp[TS * st + 128 * t:TS * st + 128 * t + 128,
                                               C + 512 * hf:C + 512 * hf + 512],
                                         vb[:, 512:1024]),
                           reads=[vb], writes=[d_vsp], dma=True)
            for qc in range(8):
                pb = PB[4 + qc % 2]
                for i in range(IC):
                    wqh = wB[i][:, 128 * qc:128 * qc + 128]
                    wql = wB[i][:, 1024 + 128 * qc:1024 + 128 * qc + 128]
                    P.emit("tensor", mm(pb[:, 0:TS], wqh, xlh[i][:, 0:TS], i == 0, False),
                           reads=[wB[i], xlh[i]], writes=[pb])
                    P.emit("tensor", mm(pb[:, 0:TS], wqh, xll[i][:, 0:TS], False, False),
                           reads=[wB[i], xll[i]], writes=[pb])
                    P.emit("tensor", mm(pb[:, 0:TS], wql, xlh[i][:, 0:TS],
                                        False, i == IC - 1),
                           reads=[wB[i], xlh[i]], writes=[pb])
                qb = kvq[qc % 3]
                q32 = ys[qc % 2]
                P.emit("vector", (lambda a, b, qc: (lambda e: e.scalar_tensor_tensor(
                    a, b, 1.0 / 32.0, baq_sb[:, qc:qc + 1].to_broadcast([128, TS]),
                    ALU.mult, ALU.add)))(q32[:, 0:TS], pb[:, 0:TS], qc),
                       reads=[pb, baq_sb], writes=[q32])
                P.emit("scalar", (lambda a, b: (lambda e: e.copy(a, b)))(
                    qb[:, 0:TS], q32[:, 0:TS]),
                       reads=[q32], writes=[qb])
                P.emit("vector", (lambda a, b, c: (lambda e: e.tensor_sub(a, b, c)))(
                    qb[:, TS:2 * TS], q32[:, 0:TS], qb[:, 0:TS]),
                       reads=[q32, qb], writes=[qb])
                P.emit("scalar", dma(a_qsp[128 * qc:128 * qc + 128, csl], qb[:, 0:TS]),
                       reads=[qb], writes=[d_qsp], dma=True)
                P.emit("scalar", dma(a_qsp[128 * qc:128 * qc + 128,
                                           T_seq + TS * st:T_seq + TS * st + TS],
                                     qb[:, TS:2 * TS]),
                       reads=[qb], writes=[d_qsp], dma=True)

        if not do_p2:
            NQ = 0
        ksets = [wA[0:8], wA[8:16]]
        wp = wB[0:8]
        vbuf = [wB[8], wB[9]]
        qj = wB[10]
        qjo = wB[15]
        oslot = wB[11:15]
        for i in range(IC):
            P.emit("sync", dma(wp[i][:, 0:1024], a_wps[128 * i:128 * i + 128, 0:C]),
                   reads=[d_wps], writes=[wp[i]], dma=True)
            P.emit("sync", dma(wp[i][:, 1024:2048], a_wps[128 * i:128 * i + 128, C:2 * C]),
                   reads=[d_wps], writes=[wp[i]], dma=True)

        for j in range(NQ):
            nkt = 2 * j + 2
            nk = 128 * nkt
            ngrp = (nkt + 3) // 4
            for i in range(IC):
                P.emit("sync", dma(qj[:, 128 * i:128 * i + 128],
                                   a_qsp[128 * i:128 * i + 128, 256 * j:256 * j + 128]),
                       reads=[d_qsp], writes=[qj], dma=True)
                P.emit("sync", dma(qj[:, 1024 + 128 * i:1024 + 128 * i + 128],
                                   a_qsp[128 * i:128 * i + 128,
                                         T_seq + 256 * j:T_seq + 256 * j + 128]),
                       reads=[d_qsp], writes=[qj], dma=True)
                P.emit("sync", dma(qjo[:, 128 * i:128 * i + 128],
                                   a_qsp[128 * i:128 * i + 128, 256 * j + 128:256 * j + 256]),
                       reads=[d_qsp], writes=[qjo], dma=True)
                P.emit("sync", dma(qjo[:, 1024 + 128 * i:1024 + 128 * i + 128],
                                   a_qsp[128 * i:128 * i + 128,
                                         T_seq + 256 * j + 128:T_seq + 256 * j + 256]),
                       reads=[d_qsp], writes=[qjo], dma=True)
            P.emit("gpsimd", lambda e: e.tensor_scalar(qj[:, :], qj[:, :], selb[:, 0:1], None, ALU.mult),
                   reads=[qj, selb], writes=[qj])
            P.emit("gpsimd", lambda e: e.tensor_scalar(qjo[:, :], qjo[:, :], selb[:, 1:2], None, ALU.mult),
                   reads=[qjo, selb], writes=[qjo])
            P.emit("gpsimd", lambda e: e.tensor_add(qj[:, :], qj[:, :], qjo[:, :]),
                   reads=[qj, qjo], writes=[qj])
            for g in range(ngrp):
                w = min(512, nk - 512 * g)
                kset = ksets[g % 2]
                for i in range(IC):
                    P.emit("sync", dma(kset[i][:, 0:w],
                                       a_ksp[128 * i:128 * i + 128, 512 * g:512 * g + w]),
                           reads=[d_ksp], writes=[kset[i]], dma=True)
                    P.emit("sync", dma(kset[i][:, 1024:1024 + w],
                                       a_ksp[128 * i:128 * i + 128,
                                             T_seq + 512 * g:T_seq + 512 * g + w]),
                           reads=[d_ksp], writes=[kset[i]], dma=True)
                sb = PB[g % 2]
                for i in range(IC):
                    qh_ = qj[:, 128 * i:128 * i + 128]
                    ql_ = qj[:, 1024 + 128 * i:1024 + 128 * i + 128]
                    P.emit("tensor", mm(sb[:, 0:w], qh_, kset[i][:, 0:w],
                                        i == 0, False),
                           reads=[qj, kset[i]], writes=[sb])
                    P.emit("tensor", mm(sb[:, 0:w], qh_, kset[i][:, 1024:1024 + w],
                                        False, False),
                           reads=[qj, kset[i]], writes=[sb])
                    P.emit("tensor", mm(sb[:, 0:w], ql_, kset[i][:, 0:w],
                                        False, i == IC - 1),
                           reads=[qj, kset[i]], writes=[sb])
                P.emit("scalar", (lambda a, b: (lambda e: e.copy(a, b)))(
                    ys[g][:, 0:w], sb[:, 0:w]),
                       reads=[sb], writes=[ys[g]])
            for (tt, mk) in ((2 * j, maskA), (2 * j + 1, maskB)):
                g4, r4 = tt // 4, tt % 4
                P.emit("vector", (lambda a, m: (lambda e: e.tensor_add(a, a, m)))(
                    ys[g4][:, 128 * r4:128 * r4 + 128], mk[:, :]),
                       reads=[ys[g4], mk], writes=[ys[g4]])
            for g in range(ngrp):
                w = min(512, nk - 512 * g)
                P.emit("vector", (lambda g, w: (lambda e: e.reduce_max(
                    rm_sb[:, g:g + 1], ys[g][:, 0:w], axis=AX.X)))(g, w),
                       reads=[ys[g]], writes=[rm_sb])
            P.emit("vector", (lambda ngrp: (lambda e: e.reduce_max(
                nm_sb[:, 0:1], rm_sb[:, 0:ngrp], axis=AX.X, negate=True)))(ngrp),
                   reads=[rm_sb], writes=[nm_sb])
            for g in range(ngrp):
                w = min(512, nk - 512 * g)
                P.emit("scalar", (lambda g, w: (lambda e: e.activation(
                    ys[g][:, 0:w], ys[g][:, 0:w], AF.Exp, bias=nm_sb[:, 0:1],
                    accum_out=sm_sb[:, g:g + 1])))(g, w),
                       reads=[ys[g], nm_sb], writes=[ys[g], sm_sb])
            P.emit("vector", (lambda ngrp: (lambda e: e.reduce_sum(
                ls_sb[:, 0:1], sm_sb[:, 0:ngrp], axis=AX.X)))(ngrp),
                   reads=[sm_sb], writes=[ls_sb])
            P.emit("vector", lambda e: e.reciprocal(rl_sb[:, 0:1], ls_sb[:, 0:1]),
                   reads=[ls_sb], writes=[rl_sb])
            for g in range(ngrp):
                w = min(512, nk - 512 * g)
                P.emit("vector", (lambda g, w: (lambda e: e.tensor_scalar(
                    ys[g][:, 0:w], ys[g][:, 0:w], rl_sb[:, 0:1], None, ALU.mult)))(g, w),
                       reads=[ys[g], rl_sb], writes=[ys[g]])
            for kt in range(nkt):
                g4, r4 = kt // 4, kt % 4
                P.emit("sync", dma(vbuf[kt % 2][:, :],
                                   a_vsp[128 * kt:128 * kt + 128, :]),
                       reads=[d_vsp], writes=[vbuf[kt % 2]], dma=True)
                tb = PB[4 + kt % 2]
                P.emit("tensor", (lambda tb, g4, r4: (lambda e: e.transpose(
                    tb[:, 0:128], ys[g4][:, 128 * r4:128 * r4 + 128], ident[:, :])))(tb, g4, r4),
                       reads=[ys[g4], ident], writes=[tb])
                P.emit("scalar", (lambda a, tb: (lambda e: e.copy(a, tb[:, 0:128])))(
                    pth[kt % 2][:, :], tb),
                       reads=[tb], writes=[pth[kt % 2]])
                P.emit("vector", (lambda a, tb, b: (lambda e: e.tensor_sub(a, tb[:, 0:128], b)))(
                    ptl[kt % 2][:, :], tb, pth[kt % 2][:, :]),
                       reads=[tb, pth[kt % 2]], writes=[ptl[kt % 2]])
                for cc in range(8):
                    ob = PB[2 + cc // 4]
                    osl = slice(128 * (cc % 4), 128 * (cc % 4) + 128)
                    vh_ = vbuf[kt % 2][:, 128 * cc:128 * cc + 128]
                    vl_ = vbuf[kt % 2][:, 1024 + 128 * cc:1024 + 128 * cc + 128]
                    P.emit("tensor", mm(ob[:, osl], vh_, pth[kt % 2][:, :],
                                        start=(kt == 0 and cc % 4 == 0), stop=False),
                           reads=[vbuf[kt % 2], pth[kt % 2]], writes=[ob])
                    P.emit("tensor", mm(ob[:, osl], vh_, ptl[kt % 2][:, :],
                                        start=False, stop=False),
                           reads=[vbuf[kt % 2], ptl[kt % 2]], writes=[ob])
                    P.emit("tensor", mm(ob[:, osl], vl_, pth[kt % 2][:, :],
                                        start=False,
                                        stop=(kt == nkt - 1 and cc % 4 == 3)),
                           reads=[vbuf[kt % 2], pth[kt % 2]], writes=[ob])
            for cc in range(8):
                ob = PB[2 + cc // 4]
                osl = slice(128 * (cc % 4), 128 * (cc % 4) + 128)
                dst = oslot[cc // 2]
                base = 1024 * (cc % 2)
                dh = slice(base + 128 * (j % 4), base + 128 * (j % 4) + 128)
                dl = slice(base + 512 + 128 * (j % 4), base + 512 + 128 * (j % 4) + 128)
                P.emit("scalar", (lambda a, b: (lambda e: e.copy(a, b)))(
                    dst[:, dh], ob[:, osl]),
                       reads=[ob], writes=[dst])
                P.emit("vector", (lambda a, b, c: (lambda e: e.tensor_sub(a, b, c)))(
                    dst[:, dl], ob[:, osl], dst[:, dh]),
                       reads=[ob, dst], writes=[dst])
            if j % 4 == 3:
                g2 = j // 4
                xm3 = a_xmsp.rearrange("p (a c) -> p a c", c=128)
                for oc in range(IC):
                    P.emit("sync", dma(xs[oc][:, 0:512].rearrange("p (a c) -> p a c", c=128),
                                       xm3[128 * oc:128 * oc + 128, 8 * g2:8 * g2 + 8:2, :]),
                           reads=[d_xmsp], writes=[xs[oc]], dma=True)
                    P.emit("sync", dma(ys[oc][:, 0:512].rearrange("p (a c) -> p a c", c=128),
                                       xm3[128 * oc:128 * oc + 128, 8 * g2 + 1:8 * g2 + 8:2, :]),
                           reads=[d_xmsp], writes=[ys[oc]], dma=True)
                for oc in range(IC):
                    P.emit("gpsimd", (lambda a: (lambda e: e.tensor_scalar(
                        a, a, selb[:, 0:1], None, ALU.mult)))(xs[oc][:, 0:512]),
                           reads=[xs[oc], selb], writes=[xs[oc]])
                    P.emit("gpsimd", (lambda a: (lambda e: e.tensor_scalar(
                        a, a, selb[:, 1:2], None, ALU.mult)))(ys[oc][:, 0:512]),
                           reads=[ys[oc], selb], writes=[ys[oc]])
                    P.emit("gpsimd", (lambda a, b: (lambda e: e.tensor_add(a, a, b)))(
                        xs[oc][:, 0:512], ys[oc][:, 0:512]),
                           reads=[xs[oc], ys[oc]], writes=[xs[oc]])
                for oc in range(IC):
                    pb = PB[6 + oc % 2]
                    for i in range(IC):
                        oh_ = oslot[i // 2][:, 1024 * (i % 2):1024 * (i % 2) + 512]
                        ol_ = oslot[i // 2][:, 1024 * (i % 2) + 512:1024 * (i % 2) + 1024]
                        wph_ = wp[i][:, 128 * oc:128 * oc + 128]
                        wpl_ = wp[i][:, 1024 + 128 * oc:1024 + 128 * oc + 128]
                        P.emit("tensor", mm(pb[:, 0:512], wph_, oh_, i == 0, False),
                               reads=[wp[i], oslot[i // 2]], writes=[pb])
                        P.emit("tensor", mm(pb[:, 0:512], wph_, ol_, False, False),
                               reads=[wp[i], oslot[i // 2]], writes=[pb])
                        P.emit("tensor", mm(pb[:, 0:512], wpl_, oh_, False, i == IC - 1),
                               reads=[wp[i], oslot[i // 2]], writes=[pb])
                    P.emit("vector", (lambda a, pb, oc: (lambda e: e.scalar_tensor_tensor(
                        a, pb[:, 0:512], bp_sb[:, oc:oc + 1], a, ALU.add, ALU.add)))(
                            xs[oc][:, 0:512], pb, oc),
                           reads=[pb, bp_sb, xs[oc]], writes=[xs[oc]])
                    P.emit("scalar", dma(a_x2sp[128 * oc:128 * oc + 128, 512 * g2:512 * g2 + 512],
                                         xs[oc][:, 0:512]),
                           reads=[xs[oc]], writes=[d_x2sp], dma=True)

        for st in range(NST3 if do_p3 else 0):
            csl = slice(TS * st, TS * st + TS)
            for i in range(IC):
                P.emit("sync", dma(xs[i][:, 0:TS], a_x2sp[128 * i:128 * i + 128, csl]),
                       reads=[d_x2sp], writes=[xs[i]], dma=True)
            emit_ln(xs, xlh, xll)
            emit_router(xlh, xll)
            emit_ffn(xlh, xll, ys)
            for oc in range(IC):
                P.emit("vector", (lambda a, b: (lambda e: e.scalar_tensor_tensor(
                    a, b, 0.5, a, ALU.mult, ALU.add)))(xs[oc][:, 0:TS], ys[oc][:, 0:TS]),
                       reads=[ys[oc], xs[oc]], writes=[xs[oc]])
            for oc in range(IC):
                P.emit("scalar", dma(a_out[128 * oc:128 * oc + 128, csl], xs[oc][:, 0:TS]),
                       reads=[xs[oc]], writes=[d_out], dma=True)

        import os as _os2
        _trunc = _os2.environ.get("TRUNC")
        if _trunc is None:
            for dest in list(P.dma_sems):
                P.barrier_dma("scalar", dest)
        with nc.Block() as block:
            P.play(block, {e: sem_auto[e] for e in ENGS},
                   trunc=None if _trunc is None else int(_trunc))

    n_inst = sum(len(v) for v in P.q.values())
    build.last_prog = P
    return nc, n_inst


def _prep_inputs(L0_attn_x, L0_attn_w_attn, L0_attn_b_attn, L0_attn_w_proj,
                 L0_attn_b_proj, L0_ffn_router_w, L0_ffn_router_b, L0_ffn_w1,
                 L0_ffn_b1, L0_ffn_w2, L0_ffn_b2, n_cores=8):
    f = np.float32
    x = np.asarray(L0_attn_x, f)
    shared = {
        "w_attn": np.ascontiguousarray(np.asarray(L0_attn_w_attn, f)),
        "ba_r": np.ascontiguousarray(np.asarray(L0_attn_b_attn, f).reshape(24, 128).T),
        "bav_row": np.ascontiguousarray(np.asarray(L0_attn_b_attn, f)[2048:].reshape(1, 1024)),
        "w_proj": np.ascontiguousarray(np.asarray(L0_attn_w_proj, f)),
        "bp_r": np.ascontiguousarray(np.asarray(L0_attn_b_proj, f).reshape(8, 128).T),
        "rw_t": np.ascontiguousarray(np.asarray(L0_ffn_router_w, f).T),
        "rb_r": np.ascontiguousarray(np.asarray(L0_ffn_router_b, f).reshape(16, 1)),
        "w1_t": np.ascontiguousarray(np.asarray(L0_ffn_w1, f).T),
        "b1_r": np.ascontiguousarray(np.asarray(L0_ffn_b1, f).reshape(32, 128).T),
        "w2_t": np.ascontiguousarray(np.asarray(L0_ffn_w2, f).T),
        "b2_r": np.ascontiguousarray(np.asarray(L0_ffn_b2, f).reshape(8, 128).T),
    }
    in_maps = []
    for c in range(n_cores):
        b, p = c // 2, c % 2
        m = dict(shared)
        m["x_t"] = np.ascontiguousarray(x[b].T)
        m["sel"] = np.array([[1.0 - p, float(p)]], f)
        in_maps.append(m)
    return in_maps


def _assemble(results, T_seq, n_cores=8):
    B = n_cores // 2
    out = np.empty((B, T_seq, C), np.float32)
    nb = T_seq // 128
    for b in range(B):
        full_t = np.empty((C, nb, 128), np.float32)
        for p in range(2):
            o = results[2 * b + p]["out_t"].reshape(C, nb // 2, 128)
            full_t[:, p::2, :] = o
        out[b] = full_t.reshape(C, T_seq).T
    return out


def kernel(**inputs):
    in_maps = _prep_inputs(**inputs)
    nc, _ = build(4096)
    res = run_bass_kernel_spmd(nc, in_maps, core_ids=list(range(8)))
    return _assemble(res.results, 4096)


# version 05
# speedup vs baseline: 1.0882x; 1.0882x over previous
import bisect
import numpy as np
from contextlib import ExitStack

import concourse.bass as bass
import concourse.mybir as mybir
from concourse.bass_utils import run_bass_kernel_spmd

F32 = mybir.dt.float32
BF16 = mybir.dt.bfloat16
I32 = mybir.dt.int32
AF = mybir.ActivationFunctionType
ALU = mybir.AluOpType
AX = mybir.AxisListType

ENGS = ("sync", "tensor", "scalar", "vector", "gpsimd")
NEG = -1.0e10
MBIG = 1.0e30
C = 1024
IC = 8
HID = 4096
NE = 16
TS = 512


class Prog:

    def __init__(self, sem_alloc):
        self.q = {e: [] for e in ENGS}
        self.marks = {e: [] for e in ENGS}
        self.inc_at = {e: set() for e in ENGS}
        self.last_w = {}
        self.readers = {}
        self.drain_pos = {e: -1 for e in ENGS}
        self.waited = {e: {} for e in ENGS}
        self.sem_alloc = sem_alloc
        self.dma_sems = {}
        self.n_dma = 0

    def _dep(self, eng, tok, want):
        if tok is None:
            return
        if tok[0] == "eng":
            _, e2, p = tok
            if e2 == eng:
                if eng != "tensor":
                    want["drain"] = max(want.get("drain", -1), p)
            else:
                k = ("auto", e2)
                want[k] = max(want.get(k, -1), p)
        else:
            _, dest, cnt = tok
            k = ("dma", dest)
            want[k] = max(want.get(k, -1), self.dma_sems[dest][1])

    def emit(self, eng, fn, reads=(), writes=(), dma=False):
        q = self.q[eng]
        want = {}
        for h in reads:
            self._dep(eng, self.last_w.get(h.name), want)
        for h in writes:
            lw = self.last_w.get(h.name)
            if not (dma and lw is not None and lw[0] == "dma" and lw[1] == h.name):
                self._dep(eng, lw, want)
            for tok in self.readers.get(h.name, ()):
                self._dep(eng, tok, want)
        self.seq = getattr(self, "seq", 0) + 1
        sq = self.seq
        dp = want.pop("drain", -1)
        if dp > self.drain_pos[eng]:
            q.append(("drain", sq))
            self.drain_pos[eng] = len(q) - 1
        for key in sorted(want, key=str):
            p = want[key]
            if key[0] == "auto":
                e2 = key[1]
                ms = self.marks[e2]
                i = bisect.bisect_left(ms, p)
                if i == len(ms):
                    ms.append(p)
                    self.inc_at[e2].add(p)
                    mp = p
                else:
                    mp = ms[i]
                if self.waited[eng].get(key, -1) < mp:
                    q.append(("wait_auto", e2, mp, sq))
                    self.waited[eng][key] = mp
            else:
                dest = key[1]
                if self.waited[eng].get(key, -1) < p:
                    q.append(("wait_dma", dest, p, sq))
                    self.waited[eng][key] = p
        pos = len(q)
        if dma:
            assert len(writes) == 1
            dest = writes[0].name
            if dest not in self.dma_sems:
                self.dma_sems[dest] = [self.sem_alloc(f"sdma_{len(self.dma_sems)}"), 0]
            self.dma_sems[dest][1] += 1
            self.n_dma += 1
            tok = ("dma", dest, self.dma_sems[dest][1])
            q.append(("op_dma", fn, dest, sq))
        else:
            tok = ("eng", eng, pos)
            q.append(("op", fn, pos, sq))
        for h in writes:
            self.last_w[h.name] = tok
            self.readers[h.name] = []
        for h in reads:
            self.readers.setdefault(h.name, []).append(tok)
        return tok

    def barrier_dma(self, eng, dest_name):
        if dest_name in self.dma_sems:
            self.q[eng].append(("wait_dma", dest_name, self.dma_sems[dest_name][1],
                                getattr(self, "seq", 0) + 10**9))

    def play(self, block, auto_sems, trunc=None):
        if trunc is not None:
            newq = {}
            dma_tot = {}
            for e in ENGS:
                kept = []
                for ent in self.q[e]:
                    sq = ent[-1] if ent[0] != "drain" else ent[1]
                    if sq > trunc:
                        continue
                    kept.append(ent)
                    if ent[0] == "op_dma":
                        dma_tot[ent[2]] = dma_tot.get(ent[2], 0) + 1
                newq[e] = kept
            self.q = newq
            for dest, tot in dma_tot.items():
                self.q["gpsimd"].append(("wait_dma", dest, tot, 0))
        rank = {e: {p: i + 1 for i, p in enumerate(self.marks[e])} for e in ENGS}

        def mk_body(ename):
            entries = self.q[ename]
            inc_at = self.inc_at[ename]

            def body(e):
                for ent in entries:
                    k = ent[0]
                    if k == "drain":
                        e.drain()
                    elif k == "wait_auto":
                        e.wait_ge(auto_sems[ent[1]], rank[ent[1]][ent[2]])
                    elif k == "wait_dma":
                        e.wait_ge(self.dma_sems[ent[1]][0], 16 * ent[2])
                    elif k == "op":
                        ins = ent[1](e)
                        if ent[2] in inc_at:
                            ins.then_inc(auto_sems[ename], 1)
                    else:
                        ent[1](e).then_inc(self.dma_sems[ent[2]][0], 16)
            return body

        block.sync(mk_body("sync"))
        block.tensor(mk_body("tensor"))
        block.scalar(mk_body("scalar"))
        block.vector(mk_body("vector"))
        block.gpsimd(mk_body("gpsimd"))


def build(T_seq=4096, p1_stages=None, do_p2=True, do_p3=True, p3_mode='full'):
    NST = T_seq // TS
    NQ = T_seq // 256
    OWN = T_seq // 2
    NST3 = OWN // TS
    NKT = T_seq // 128

    nc = bass.Bass("TRN2", target_bir_lowering=False,
                   dynamic_dma_scratch_size=1024)

    d_xt = nc.dram_tensor("x_t", [C, T_seq], F32, kind="ExternalInput")
    d_wa = nc.dram_tensor("w_attn", [C, 3 * C], F32, kind="ExternalInput")
    d_ba = nc.dram_tensor("ba_r", [128, 24], F32, kind="ExternalInput")
    d_bav = nc.dram_tensor("bav_row", [1, C], F32, kind="ExternalInput")
    d_wp = nc.dram_tensor("w_proj", [C, C], F32, kind="ExternalInput")
    d_bp = nc.dram_tensor("bp_r", [128, 8], F32, kind="ExternalInput")
    d_rwt = nc.dram_tensor("rw_t", [C, NE], F32, kind="ExternalInput")
    d_rb = nc.dram_tensor("rb_r", [NE, 1], F32, kind="ExternalInput")
    d_w1t = nc.dram_tensor("w1_t", [C, HID], F32, kind="ExternalInput")
    d_b1 = nc.dram_tensor("b1_r", [128, 32], F32, kind="ExternalInput")
    d_w2t = nc.dram_tensor("w2_t", [HID, C], F32, kind="ExternalInput")
    d_b2 = nc.dram_tensor("b2_r", [128, 8], F32, kind="ExternalInput")
    d_sel = nc.dram_tensor("sel", [1, 2], F32, kind="ExternalInput")
    d_out = nc.dram_tensor("out_t", [C, OWN], F32, kind="ExternalOutput")

    d_ksp = nc.dram_tensor("k_sp", [C, 2 * T_seq], BF16, kind="Internal")
    d_vsp = nc.dram_tensor("v_sp", [T_seq, 2 * C], BF16, kind="Internal")
    d_qsp = nc.dram_tensor("q_sp", [C, 2 * T_seq], BF16, kind="Internal")
    d_w1s = nc.dram_tensor("w1_s", [C, 2 * HID], BF16, kind="Internal")
    d_w2s = nc.dram_tensor("w2_s", [HID, 2 * C], BF16, kind="Internal")
    d_was = nc.dram_tensor("wa_s", [C, 6 * C], BF16, kind="Internal")
    d_wps = nc.dram_tensor("wp_s", [C, 2 * C], BF16, kind="Internal")
    d_xmsp = nc.dram_tensor("xm_sp", [C, T_seq], F32, kind="Internal")
    d_x2sp = nc.dram_tensor("x2_sp", [C, OWN], F32, kind="Internal")

    a_xt = d_xt.ap(); a_wa = d_wa.ap(); a_wp = d_wp.ap(); a_rwt = d_rwt.ap()
    a_w1t = d_w1t.ap(); a_w2t = d_w2t.ap()
    a_ksp = d_ksp.ap(); a_vsp = d_vsp.ap(); a_qsp = d_qsp.ap()
    a_xmsp = d_xmsp.ap(); a_x2sp = d_x2sp.ap(); a_out = d_out.ap()
    a_w1s = d_w1s.ap(); a_w2s = d_w2s.ap(); a_was = d_was.ap(); a_wps = d_wps.ap()


    with ExitStack() as ctx:
        E = ctx.enter_context
        xs = [E(nc.sbuf_tensor(f"xs{i}", [128, TS], F32)) for i in range(IC)]
        xlh = [E(nc.sbuf_tensor(f"xlh{i}", [128, TS], BF16)) for i in range(IC)]
        xll = [E(nc.sbuf_tensor(f"xll{i}", [128, TS], BF16)) for i in range(IC)]
        ys = [E(nc.sbuf_tensor(f"ys{i}", [128, TS], F32)) for i in range(IC)]
        wA = [E(nc.sbuf_tensor(f"wA{i}", [128, 2048], BF16)) for i in range(16)]
        wB = [E(nc.sbuf_tensor(f"wB{i}", [128, 2048], BF16)) for i in range(16)]
        hsb = [E(nc.sbuf_tensor(f"hsb{i}", [128, 256], F32)) for i in range(2)]
        hh = [E(nc.sbuf_tensor(f"hh{i}", [128, 256], BF16)) for i in range(2)]
        hl = [E(nc.sbuf_tensor(f"hl{i}", [128, 256], BF16)) for i in range(2)]
        kvq = [E(nc.sbuf_tensor(f"kvq{i}", [128, 2 * TS], BF16)) for i in range(3)]
        pth = [E(nc.sbuf_tensor(f"pth{i}", [128, 128], BF16)) for i in range(2)]
        ptl = [E(nc.sbuf_tensor(f"ptl{i}", [128, 128], BF16)) for i in range(2)]
        rwt_sb = [E(nc.sbuf_tensor(f"rwt{i}", [128, NE], F32)) for i in range(IC)]
        rwh = [E(nc.sbuf_tensor(f"rwh{i}", [128, NE], BF16)) for i in range(IC)]
        rwl = [E(nc.sbuf_tensor(f"rwl{i}", [128, NE], BF16)) for i in range(IC)]
        rowbuf = E(nc.sbuf_tensor("rowbuf", [1, 2048], F32))
        io = E(nc.sbuf_tensor("io", [128, 128], F32))
        esel = E(nc.sbuf_tensor("esel", [16, 2048], BF16))
        ident = E(nc.sbuf_tensor("ident", [128, 128], F32))
        trib = E(nc.sbuf_tensor("trib", [128, 128], F32))
        fullm = E(nc.sbuf_tensor("fullm", [128, 128], F32))
        ones_c = E(nc.sbuf_tensor("ones_c", [128, 1], F32))
        ones_r = E(nc.sbuf_tensor("ones_r", [1, 128], F32))
        selb = E(nc.sbuf_tensor("selb", [128, 2], F32))
        sel_sb = E(nc.sbuf_tensor("sel_sb", [1, 2], F32))
        lg_sb = E(nc.sbuf_tensor("lg_sb", [NE, TS], F32))
        ltm = E(nc.sbuf_tensor("ltm", [128, 64], F32))
        top8 = E(nc.sbuf_tensor("top8", [128, 32], F32))
        mtm = E(nc.sbuf_tensor("mtm", [128, 64], F32))
        mfm = E(nc.sbuf_tensor("mfm", [NE, TS], F32))
        mmfm = E(nc.sbuf_tensor("mmfm", [NE, TS], BF16))
        ba_sb = E(nc.sbuf_tensor("ba_sb", [128, 24], F32))
        baq_sb = E(nc.sbuf_tensor("baq_sb", [128, 8], F32))
        bav_sb = E(nc.sbuf_tensor("bav_sb", [1, C], F32))
        bp_sb = E(nc.sbuf_tensor("bp_sb", [128, 8], F32))
        b1_sb = E(nc.sbuf_tensor("b1_sb", [128, 32], F32))
        b2_sb = E(nc.sbuf_tensor("b2_sb", [128, 8], F32))
        rb_sb = E(nc.sbuf_tensor("rb_sb", [NE, 1], F32))
        rm_sb = E(nc.sbuf_tensor("rm_sb", [128, 8], F32))
        sm_sb = E(nc.sbuf_tensor("sm_sb", [128, 8], F32))
        nm_sb = E(nc.sbuf_tensor("nm_sb", [128, 1], F32))
        ls_sb = E(nc.sbuf_tensor("ls_sb", [128, 1], F32))
        rl_sb = E(nc.sbuf_tensor("rl_sb", [128, 1], F32))
        eps_sb = E(nc.sbuf_tensor("eps_sb", [1, 1], F32))

        PB = [E(nc.psum_tensor(f"pb{i}", [128, 512], F32)) for i in range(8)]

        sem_auto = {e: E(nc.semaphore(f"sa_{e}")) for e in ENGS}
        P = Prog(lambda name: E(nc.semaphore(name)))

        def mm(out_ap, l_ap, r_ap, start, stop):
            return lambda e: e.matmul(out_ap, l_ap, r_ap, start=start, stop=stop)

        def dma(out_ap, in_ap):
            return lambda e: e.dma_start(out=out_ap, in_=in_ap)

        P.emit("sync", dma(sel_sb[:, :], d_sel.ap()), reads=[d_sel], writes=[sel_sb], dma=True)
        P.emit("sync", dma(ba_sb[:, :], d_ba.ap()), reads=[d_ba], writes=[ba_sb], dma=True)
        P.emit("sync", dma(bav_sb[:, :], d_bav.ap()), reads=[d_bav], writes=[bav_sb], dma=True)
        P.emit("sync", dma(bp_sb[:, :], d_bp.ap()), reads=[d_bp], writes=[bp_sb], dma=True)
        P.emit("sync", dma(b1_sb[:, :], d_b1.ap()), reads=[d_b1], writes=[b1_sb], dma=True)
        P.emit("sync", dma(b2_sb[:, :], d_b2.ap()), reads=[d_b2], writes=[b2_sb], dma=True)
        P.emit("sync", dma(rb_sb[:, :], d_rb.ap()), reads=[d_rb], writes=[rb_sb], dma=True)
        for i in range(IC):
            P.emit("sync", dma(rwt_sb[i][:, :], a_rwt[128 * i:128 * i + 128, :]),
                   reads=[d_rwt], writes=[rwt_sb[i]], dma=True)

        P.emit("gpsimd", lambda e: e.iota(io[:, :], pattern=[[1, 128]], base=0,
                                          channel_multiplier=-1,
                                          allow_small_or_imprecise_dtypes=True),
               writes=[io])
        for q4 in range(4):
            P.emit("gpsimd", (lambda q4: (lambda e: e.iota(
                xs[0][0:16, 0:512], pattern=[[1, 4], [0, 128]], base=4 * q4,
                channel_multiplier=-1, allow_small_or_imprecise_dtypes=True)))(q4),
                   writes=[xs[0]])
            P.emit("vector", (lambda q4: (lambda e: e.tensor_scalar(
                esel[:, 512 * q4:512 * q4 + 512], xs[0][0:16, 0:512],
                0.0, None, ALU.is_equal)))(q4),
                   reads=[xs[0]], writes=[esel])
        P.emit("gpsimd", lambda e: e.memset(ones_c[:, :], 1.0), writes=[ones_c])
        P.emit("gpsimd", lambda e: e.memset(ones_r[:, :], 1.0), writes=[ones_r])
        P.emit("vector", lambda e: e.memset(fullm[:, :], NEG), writes=[fullm])
        P.emit("vector", lambda e: e.memset(eps_sb[:, :], 1e-5), writes=[eps_sb])
        P.emit("vector", lambda e: e.tensor_scalar(ident[:, :], io[:, :], 0.0, None, ALU.is_equal),
               reads=[io], writes=[ident])
        P.emit("vector", lambda e: e.tensor_scalar(trib[:, :], io[:, :], 0.0, NEG, ALU.is_gt, ALU.mult),
               reads=[io], writes=[trib])
        P.emit("tensor", mm(PB[0][:, 0:2], ones_r[:, :], sel_sb[:, :], True, True),
               reads=[ones_r, sel_sb], writes=[PB[0]])
        P.emit("scalar", lambda e: e.copy(selb[:, :], PB[0][:, 0:2]), reads=[PB[0]], writes=[selb])
        P.emit("vector", lambda e: e.tensor_scalar(io[:, :], trib[:, :], selb[:, 1:2], None, ALU.mult),
               reads=[trib, selb], writes=[io])
        P.emit("vector", lambda e: e.tensor_scalar(fullm[:, :], fullm[:, :], selb[:, 0:1], None, ALU.mult),
               reads=[fullm, selb], writes=[fullm])
        P.emit("vector", lambda e: e.tensor_add(fullm[:, :], fullm[:, :], io[:, :]),
               reads=[fullm, io], writes=[fullm])
        P.emit("vector", lambda e: e.tensor_scalar(trib[:, :], trib[:, :], selb[:, 0:1], None, ALU.mult),
               reads=[trib, selb], writes=[trib])
        maskA, maskB = trib, fullm
        P.emit("vector", lambda e: e.tensor_scalar(baq_sb[:, :], ba_sb[:, 0:8], 1.0 / 32.0, None, ALU.mult),
               reads=[ba_sb], writes=[baq_sb])

        _pc = [0]

        def emit_split(src_ap, src_h, hi_ap_fn, lo_ap_fn, rows, cols):
            for r in range(rows // 128):
                for cc in range(cols // 512):
                    st8 = xs[_pc[0] % 4]
                    kb8 = kvq[_pc[0] % 3]
                    _pc[0] += 1
                    P.emit("sync", dma(st8[:, 0:512],
                                       src_ap[128 * r:128 * r + 128, 512 * cc:512 * cc + 512]),
                           reads=[src_h], writes=[st8], dma=True)
                    P.emit("vector", (lambda a, b: (lambda e: e.tensor_copy(a, b)))(
                        kb8[:, 0:512], st8[:, 0:512]),
                           reads=[st8], writes=[kb8])
                    P.emit("vector", (lambda a, b, c: (lambda e: e.tensor_sub(a, b, c)))(
                        kb8[:, 512:1024], st8[:, 0:512], kb8[:, 0:512]),
                           reads=[st8, kb8], writes=[kb8])
                    P.emit("scalar", dma(hi_ap_fn(r, cc), kb8[:, 0:512]),
                           reads=[kb8], writes=[hi_ap_fn.dest], dma=True)
                    P.emit("scalar", dma(lo_ap_fn(r, cc), kb8[:, 512:1024]),
                           reads=[kb8], writes=[lo_ap_fn.dest], dma=True)

        def _mk(dest_h, base_ap, lo_off):
            def hi(r, cc):
                return base_ap[128 * r:128 * r + 128, 512 * cc:512 * cc + 512]
            def lo(r, cc):
                return base_ap[128 * r:128 * r + 128, lo_off + 512 * cc:lo_off + 512 * cc + 512]
            hi.dest = dest_h
            lo.dest = dest_h
            return hi, lo

        h1, l1 = _mk(d_w1s, a_w1s, HID)
        emit_split(a_w1t, d_w1t, h1, l1, C, HID)
        h2, l2 = _mk(d_w2s, a_w2s, C)
        emit_split(a_w2t, d_w2t, h2, l2, HID, C)
        h3, l3 = _mk(d_was, a_was, 3 * C)
        emit_split(a_wa, d_wa, h3, l3, C, 3 * C)
        h4, l4 = _mk(d_wps, a_wps, C)
        emit_split(a_wp, d_wp, h4, l4, C, C)
        for i in range(IC):
            P.emit("vector", (lambda i: (lambda e: e.tensor_copy(rwh[i][:, :], rwt_sb[i][:, :])))(i),
                   reads=[rwt_sb[i]], writes=[rwh[i]])
            P.emit("vector", (lambda i: (lambda e: e.tensor_sub(rwl[i][:, :], rwt_sb[i][:, :], rwh[i][:, :])))(i),
                   reads=[rwt_sb[i], rwh[i]], writes=[rwl[i]])

        RB_MU, RB_E2, RB_T, RB_STD, RB_RSTD, RB_MRS = (
            (0, 512), (512, 512), (1024, 512), (1024, 512), (1536, 512), (0, 512))

        def rb(which):
            o, w = which
            return rowbuf[0:1, o:o + TS]

        def emit_ln(src, dsth, dstl):
            for i in range(IC):
                P.emit("tensor", mm(PB[6][0:1, 0:TS], ones_c[:, :], src[i][:, 0:TS],
                                    i == 0, i == IC - 1),
                       reads=[ones_c, src[i]], writes=[PB[6]])
                P.emit("scalar", (lambda a, b: (lambda e: e.square(a, b)))(
                    ys[i % 2][:, 0:TS], src[i][:, 0:TS]),
                       reads=[src[i]], writes=[ys[i % 2]])
                P.emit("tensor", mm(PB[7][0:1, 0:TS], ones_c[:, :], ys[i % 2][:, 0:TS],
                                    i == 0, i == IC - 1),
                       reads=[ones_c, ys[i % 2]], writes=[PB[7]])
            P.emit("vector", lambda e: e.tensor_scalar(rb(RB_MU), PB[6][0:1, 0:TS], 1.0 / C, None, ALU.mult),
                   reads=[PB[6]], writes=[rowbuf])
            P.emit("vector", lambda e: e.tensor_scalar(rb(RB_E2), PB[7][0:1, 0:TS], 1.0 / C, None, ALU.mult),
                   reads=[PB[7]], writes=[rowbuf])
            P.emit("vector", lambda e: e.tensor_mul(rb(RB_T), rb(RB_MU), rb(RB_MU)),
                   reads=[rowbuf], writes=[rowbuf])
            P.emit("vector", lambda e: e.tensor_sub(rb(RB_E2), rb(RB_E2), rb(RB_T)),
                   reads=[rowbuf], writes=[rowbuf])
            P.emit("scalar", lambda e: e.activation(rb(RB_STD), rb(RB_E2), AF.Sqrt,
                                                    bias=eps_sb[0:1, 0:1]),
                   reads=[rowbuf, eps_sb], writes=[rowbuf])
            P.emit("vector", lambda e: e.reciprocal(rb(RB_RSTD), rb(RB_STD)),
                   reads=[rowbuf], writes=[rowbuf])
            P.emit("vector", lambda e: e.tensor_mul(rb(RB_MRS), rb(RB_MU), rb(RB_RSTD)),
                   reads=[rowbuf], writes=[rowbuf])
            P.emit("tensor", mm(PB[6][:, 0:TS], ones_r[:, :], rb(RB_RSTD), True, True),
                   reads=[ones_r, rowbuf], writes=[PB[6]])
            P.emit("tensor", mm(PB[7][:, 0:TS], ones_r[:, :], rb(RB_MRS), True, True),
                   reads=[ones_r, rowbuf], writes=[PB[7]])
            for i in range(IC):
                P.emit("vector", (lambda a, b: (lambda e: e.tensor_mul(a, b, PB[6][:, 0:TS])))(
                    ys[i][:, 0:TS], src[i][:, 0:TS]),
                       reads=[src[i], PB[6]], writes=[ys[i]])
            for i in range(IC):
                P.emit("vector", (lambda a: (lambda e: e.tensor_sub(a, a, PB[7][:, 0:TS])))(
                    ys[i][:, 0:TS]),
                       reads=[ys[i], PB[7]], writes=[ys[i]])
            for i in range(IC):
                P.emit("vector", (lambda a, b: (lambda e: e.tensor_copy(a, b)))(
                    dsth[i][:, 0:TS], ys[i][:, 0:TS]),
                       reads=[ys[i]], writes=[dsth[i]])
            for i in range(IC):
                P.emit("vector", (lambda a, b, c: (lambda e: e.tensor_sub(a, b, c)))(
                    dstl[i][:, 0:TS], ys[i][:, 0:TS], dsth[i][:, 0:TS]),
                       reads=[ys[i], dsth[i]], writes=[dstl[i]])

        def emit_router(srch, srcl):
            for i in range(IC):
                P.emit("tensor", mm(PB[5][0:NE, 0:TS], rwh[i][:, :], srch[i][:, 0:TS],
                                    i == 0, False),
                       reads=[rwh[i], srch[i]], writes=[PB[5]])
                P.emit("tensor", mm(PB[5][0:NE, 0:TS], rwh[i][:, :], srcl[i][:, 0:TS],
                                    False, False),
                       reads=[rwh[i], srcl[i]], writes=[PB[5]])
                P.emit("tensor", mm(PB[5][0:NE, 0:TS], rwl[i][:, :], srch[i][:, 0:TS],
                                    False, i == IC - 1),
                       reads=[rwl[i], srch[i]], writes=[PB[5]])
            P.emit("scalar", lambda e: e.activation(lg_sb[0:NE, 0:TS], PB[5][0:NE, 0:TS],
                                                    AF.Identity, bias=rb_sb[0:NE, 0:1]),
                   reads=[PB[5], rb_sb], writes=[lg_sb])
            nt = TS // 128
            for t in range(nt):
                P.emit("tensor", (lambda t: (lambda e: e.transpose(
                    PB[6][:, 16 * t:16 * t + 16], lg_sb[0:NE, 128 * t:128 * t + 128],
                    ident[0:NE, 0:NE])))(t),
                       reads=[lg_sb, ident], writes=[PB[6]])
            P.emit("scalar", lambda e: e.copy(ltm[:, 0:16 * nt], PB[6][:, 0:16 * nt]),
                   reads=[PB[6]], writes=[ltm])
            for t in range(nt):
                P.emit("vector", (lambda t: (lambda e: e.max(
                    out=top8[:, 8 * t:8 * t + 8], in_=ltm[:, 16 * t:16 * t + 16])))(t),
                       reads=[ltm], writes=[top8])
            for t in range(nt):
                P.emit("vector", (lambda t: (lambda e: e.tensor_scalar(
                    mtm[:, 16 * t:16 * t + 16], ltm[:, 16 * t:16 * t + 16],
                    top8[:, 8 * t + 3:8 * t + 4], None, ALU.is_ge)))(t),
                       reads=[ltm, top8], writes=[mtm])
            for t in range(nt):
                P.emit("tensor", (lambda t: (lambda e: e.transpose(
                    PB[7][0:NE, 128 * t:128 * t + 128], mtm[:, 16 * t:16 * t + 16],
                    ident[:, :])))(t),
                       reads=[mtm, ident], writes=[PB[7]])
            P.emit("scalar", lambda e: e.copy(mfm[0:NE, 0:TS], PB[7][0:NE, 0:TS]),
                   reads=[PB[7]], writes=[mfm])
            P.emit("vector", lambda e: e.tensor_scalar(mmfm[0:NE, 0:TS], mfm[0:NE, 0:TS],
                                                       MBIG, -MBIG, ALU.mult, ALU.add),
                   reads=[mfm], writes=[mmfm])

        def emit_ffn(srch, srcl, dst):
            for q in range(4):
                buf = wA if q % 2 == 0 else wB
                for i in range(IC):
                    P.emit("sync", dma(buf[i][:, 0:1024],
                                       a_w1s[128 * i:128 * i + 128, 1024 * q:1024 * q + 1024]),
                           reads=[d_w1s], writes=[buf[i]], dma=True)
                    P.emit("sync", dma(buf[i][:, 1024:2048],
                                       a_w1s[128 * i:128 * i + 128,
                                             HID + 1024 * q:HID + 1024 * q + 1024]),
                           reads=[d_w1s], writes=[buf[i]], dma=True)
                for hcl in range(8):
                    rr = 1024 * q + 128 * hcl
                    P.emit("sync", dma(buf[8 + hcl][:, 0:1024], a_w2s[rr:rr + 128, 0:1024]),
                           reads=[d_w2s], writes=[buf[8 + hcl]], dma=True)
                    P.emit("sync", dma(buf[8 + hcl][:, 1024:2048], a_w2s[rr:rr + 128, 1024:2048]),
                           reads=[d_w2s], writes=[buf[8 + hcl]], dma=True)
                for g in range(TS // 256):
                    gsl = slice(256 * g, 256 * g + 256)

                    def emit_y(hcl):
                        hhc, hlc = hh[hcl % 2], hl[hcl % 2]
                        for oc in range(8):
                            yb = PB[oc // 2]
                            ysl = slice(256 * (oc % 2), 256 * (oc % 2) + 256)
                            w2h = buf[8 + hcl][:, 128 * oc:128 * oc + 128]
                            w2l = buf[8 + hcl][:, 1024 + 128 * oc:1024 + 128 * oc + 128]
                            P.emit("tensor", mm(yb[:, ysl], w2h, hhc[:, 0:256],
                                                start=(hcl == 0 and oc % 2 == 0), stop=False),
                                   reads=[buf[8 + hcl], hhc], writes=[yb])
                            P.emit("tensor", mm(yb[:, ysl], w2h, hlc[:, 0:256],
                                                start=False, stop=False),
                                   reads=[buf[8 + hcl], hlc], writes=[yb])
                            P.emit("tensor", mm(yb[:, ysl], w2l, hhc[:, 0:256],
                                                start=False,
                                                stop=(hcl == 7 and oc % 2 == 1)),
                                   reads=[buf[8 + hcl], hhc], writes=[yb])

                    for hcl in range(8):
                        hc = 8 * q + hcl
                        hb = PB[4 + hcl % 2]
                        for i in range(IC):
                            w1h = buf[i][:, 128 * hcl:128 * hcl + 128]
                            w1l = buf[i][:, 1024 + 128 * hcl:1024 + 128 * hcl + 128]
                            P.emit("tensor", mm(hb[:, 0:256], w1h, srch[i][:, gsl],
                                                i == 0, False),
                                   reads=[buf[i], srch[i]], writes=[hb])
                            P.emit("tensor", mm(hb[:, 0:256], w1h, srcl[i][:, gsl],
                                                False, False),
                                   reads=[buf[i], srcl[i]], writes=[hb])
                            P.emit("tensor", mm(hb[:, 0:256], w1l, srch[i][:, gsl],
                                                False, False),
                                   reads=[buf[i], srch[i]], writes=[hb])
                        P.emit("tensor", mm(hb[:, 0:256], esel[:, 128 * (hc // 2):128 * (hc // 2) + 128],
                                            mmfm[0:NE, gsl], False, True),
                               reads=[esel, mmfm], writes=[hb])
                        hs = hsb[hcl % 2]
                        P.emit("scalar", (lambda a, b, hc: (lambda e: e.activation(
                            a, b, AF.Relu, bias=b1_sb[:, hc:hc + 1])))(
                                hs[:, 0:256], hb[:, 0:256], hc),
                               reads=[hb, b1_sb], writes=[hs])
                        hhc, hlc = hh[hcl % 2], hl[hcl % 2]
                        P.emit("vector", (lambda a, b: (lambda e: e.tensor_copy(a, b)))(
                            hhc[:, 0:256], hs[:, 0:256]),
                               reads=[hs], writes=[hhc])
                        P.emit("vector", (lambda a, b, c: (lambda e: e.tensor_sub(a, b, c)))(
                            hlc[:, 0:256], hs[:, 0:256], hhc[:, 0:256]),
                               reads=[hs, hhc], writes=[hlc])
                        if hcl > 0:
                            emit_y(hcl - 1)
                    emit_y(7)
                    for oc in range(8):
                        yb = PB[oc // 2]
                        ysl = slice(256 * (oc % 2), 256 * (oc % 2) + 256)
                        if q == 0:
                            P.emit("scalar", (lambda a, b, oc: (lambda e: e.activation(
                                a, b, AF.Identity, bias=b2_sb[:, oc:oc + 1])))(
                                    dst[oc][:, gsl], yb[:, ysl], oc),
                                   reads=[yb, b2_sb], writes=[dst[oc]])
                        else:
                            P.emit("vector", (lambda a, b: (lambda e: e.tensor_add(a, a, b)))(
                                dst[oc][:, gsl], yb[:, ysl]),
                                   reads=[yb, dst[oc]], writes=[dst[oc]])

        for st in range(NST if p1_stages is None else p1_stages):
            csl = slice(TS * st, TS * st + TS)
            for i in range(IC):
                P.emit("sync", dma(xs[i][:, 0:TS], a_xt[128 * i:128 * i + 128, csl]),
                       reads=[d_xt], writes=[xs[i]], dma=True)
            emit_ln(xs, xlh, xll)
            emit_router(xlh, xll)
            emit_ffn(xlh, xll, ys)
            for oc in range(IC):
                P.emit("vector", (lambda a, b: (lambda e: e.scalar_tensor_tensor(
                    a, b, 0.5, a, ALU.mult, ALU.add)))(xs[oc][:, 0:TS], ys[oc][:, 0:TS]),
                       reads=[ys[oc], xs[oc]], writes=[xs[oc]])
                P.emit("scalar", dma(a_xmsp[128 * oc:128 * oc + 128, csl], xs[oc][:, 0:TS]),
                       reads=[xs[oc]], writes=[d_xmsp], dma=True)
            emit_ln(xs, xlh, xll)
            for i in range(IC):
                P.emit("sync", dma(wA[i][:, 0:1024], a_was[128 * i:128 * i + 128, C:2 * C]),
                       reads=[d_was], writes=[wA[i]], dma=True)
                P.emit("sync", dma(wA[i][:, 1024:2048],
                                   a_was[128 * i:128 * i + 128, 4 * C:5 * C]),
                       reads=[d_was], writes=[wA[i]], dma=True)
            for i in range(IC):
                P.emit("sync", dma(wA[8 + i][:, 0:1024], a_was[128 * i:128 * i + 128, 2 * C:3 * C]),
                       reads=[d_was], writes=[wA[8 + i]], dma=True)
                P.emit("sync", dma(wA[8 + i][:, 1024:2048],
                                   a_was[128 * i:128 * i + 128, 5 * C:6 * C]),
                       reads=[d_was], writes=[wA[8 + i]], dma=True)
            for i in range(IC):
                P.emit("sync", dma(wB[i][:, 0:1024], a_was[128 * i:128 * i + 128, 0:C]),
                       reads=[d_was], writes=[wB[i]], dma=True)
                P.emit("sync", dma(wB[i][:, 1024:2048],
                                   a_was[128 * i:128 * i + 128, 3 * C:4 * C]),
                       reads=[d_was], writes=[wB[i]], dma=True)
            for kc in range(8):
                pb = PB[kc % 2]
                for i in range(IC):
                    wh_ = wA[i][:, 128 * kc:128 * kc + 128]
                    wl_ = wA[i][:, 1024 + 128 * kc:1024 + 128 * kc + 128]
                    P.emit("tensor", mm(pb[:, 0:TS], wh_, xlh[i][:, 0:TS], i == 0, False),
                           reads=[wA[i], xlh[i]], writes=[pb])
                    P.emit("tensor", mm(pb[:, 0:TS], wh_, xll[i][:, 0:TS], False, False),
                           reads=[wA[i], xll[i]], writes=[pb])
                    P.emit("tensor", mm(pb[:, 0:TS], wl_, xlh[i][:, 0:TS],
                                        False, i == IC - 1),
                           reads=[wA[i], xlh[i]], writes=[pb])
                kb = kvq[kc % 3]
                P.emit("scalar", (lambda a, b, kc: (lambda e: e.activation(
                    a, b, AF.Identity, bias=ba_sb[:, 8 + kc:9 + kc])))(
                        kb[:, 0:TS], pb[:, 0:TS], kc),
                       reads=[pb, ba_sb], writes=[kb])
                P.emit("vector", (lambda a, b, c, kc: (lambda e: e.scalar_tensor_tensor(
                    a, b, ba_sb[:, 8 + kc:9 + kc], c, ALU.add, ALU.subtract)))(
                        kb[:, TS:2 * TS], pb[:, 0:TS], kb[:, 0:TS], kc),
                       reads=[pb, ba_sb, kb], writes=[kb])
                P.emit("scalar", dma(a_ksp[128 * kc:128 * kc + 128, csl], kb[:, 0:TS]),
                       reads=[kb], writes=[d_ksp], dma=True)
                P.emit("scalar", dma(a_ksp[128 * kc:128 * kc + 128,
                                           T_seq + TS * st:T_seq + TS * st + TS],
                                     kb[:, TS:2 * TS]),
                       reads=[kb], writes=[d_ksp], dma=True)
            for t in range(TS // 128):
                for hf in range(2):
                    pb = PB[2 + (2 * t + hf) % 2]
                    for i in range(IC):
                        xh_t = xlh[i][:, 128 * t:128 * t + 128]
                        xl_t = xll[i][:, 128 * t:128 * t + 128]
                        wvh = wA[8 + i][:, 512 * hf:512 * hf + 512]
                        wvl = wA[8 + i][:, 1024 + 512 * hf:1024 + 512 * hf + 512]
                        P.emit("tensor", mm(pb[:, 0:512], xh_t, wvh, i == 0, False),
                               reads=[xlh[i], wA[8 + i]], writes=[pb])
                        P.emit("tensor", mm(pb[:, 0:512], xl_t, wvh, False, False),
                               reads=[xll[i], wA[8 + i]], writes=[pb])
                        P.emit("tensor", mm(pb[:, 0:512], xh_t, wvl, False, False),
                               reads=[xlh[i], wA[8 + i]], writes=[pb])
                    P.emit("tensor", mm(pb[:, 0:512], ones_r[:, :],
                                        bav_sb[0:1, 512 * hf:512 * hf + 512], False, True),
                           reads=[ones_r, bav_sb], writes=[pb])
                    vb = kvq[(2 * t + hf) % 3]
                    P.emit("scalar", (lambda a, b: (lambda e: e.copy(a, b)))(
                        vb[:, 0:512], pb[:, 0:512]),
                           reads=[pb], writes=[vb])
                    P.emit("vector", (lambda a, b, c: (lambda e: e.tensor_sub(a, b, c)))(
                        vb[:, 512:1024], pb[:, 0:512], vb[:, 0:512]),
                           reads=[pb, vb], writes=[vb])
                    P.emit("scalar", dma(a_vsp[TS * st + 128 * t:TS * st + 128 * t + 128,
                                               512 * hf:512 * hf + 512], vb[:, 0:512]),
                           reads=[vb], writes=[d_vsp], dma=True)
                    P.emit("scalar", dma(a_vsp[TS * st + 128 * t:TS * st + 128 * t + 128,
                                               C + 512 * hf:C + 512 * hf + 512],
                                         vb[:, 512:1024]),
                           reads=[vb], writes=[d_vsp], dma=True)
            for qc in range(8):
                pb = PB[4 + qc % 2]
                for i in range(IC):
                    wqh = wB[i][:, 128 * qc:128 * qc + 128]
                    wql = wB[i][:, 1024 + 128 * qc:1024 + 128 * qc + 128]
                    P.emit("tensor", mm(pb[:, 0:TS], wqh, xlh[i][:, 0:TS], i == 0, False),
                           reads=[wB[i], xlh[i]], writes=[pb])
                    P.emit("tensor", mm(pb[:, 0:TS], wqh, xll[i][:, 0:TS], False, False),
                           reads=[wB[i], xll[i]], writes=[pb])
                    P.emit("tensor", mm(pb[:, 0:TS], wql, xlh[i][:, 0:TS],
                                        False, i == IC - 1),
                           reads=[wB[i], xlh[i]], writes=[pb])
                qb = kvq[qc % 3]
                q32 = ys[qc % 2]
                P.emit("vector", (lambda a, b, qc: (lambda e: e.scalar_tensor_tensor(
                    a, b, 1.0 / 32.0, baq_sb[:, qc:qc + 1].to_broadcast([128, TS]),
                    ALU.mult, ALU.add)))(q32[:, 0:TS], pb[:, 0:TS], qc),
                       reads=[pb, baq_sb], writes=[q32])
                P.emit("scalar", (lambda a, b: (lambda e: e.copy(a, b)))(
                    qb[:, 0:TS], q32[:, 0:TS]),
                       reads=[q32], writes=[qb])
                P.emit("vector", (lambda a, b, c: (lambda e: e.tensor_sub(a, b, c)))(
                    qb[:, TS:2 * TS], q32[:, 0:TS], qb[:, 0:TS]),
                       reads=[q32, qb], writes=[qb])
                P.emit("scalar", dma(a_qsp[128 * qc:128 * qc + 128, csl], qb[:, 0:TS]),
                       reads=[qb], writes=[d_qsp], dma=True)
                P.emit("scalar", dma(a_qsp[128 * qc:128 * qc + 128,
                                           T_seq + TS * st:T_seq + TS * st + TS],
                                     qb[:, TS:2 * TS]),
                       reads=[qb], writes=[d_qsp], dma=True)

        if not do_p2:
            NQ = 0
        ksets = [wA[0:8], wA[8:16]]
        wp = wB[0:8]
        vbuf = [wB[8], wB[9]]
        qjb = [wB[10], wB[15]]
        oslot = wB[11:15]
        for i in range(IC):
            P.emit("sync", dma(wp[i][:, 0:1024], a_wps[128 * i:128 * i + 128, 0:C]),
                   reads=[d_wps], writes=[wp[i]], dma=True)
            P.emit("sync", dma(wp[i][:, 1024:2048], a_wps[128 * i:128 * i + 128, C:2 * C]),
                   reads=[d_wps], writes=[wp[i]], dma=True)

        def s_part(j):
            nkt = 2 * j + 2
            nk = 128 * nkt
            ngrp = (nkt + 3) // 4
            qj = qjb[j % 2]
            qoh, qol = kvq[0], kvq[1]
            for i in range(IC):
                P.emit("sync", dma(qj[:, 128 * i:128 * i + 128],
                                   a_qsp[128 * i:128 * i + 128, 256 * j:256 * j + 128]),
                       reads=[d_qsp], writes=[qj], dma=True)
                P.emit("sync", dma(qj[:, 1024 + 128 * i:1024 + 128 * i + 128],
                                   a_qsp[128 * i:128 * i + 128,
                                         T_seq + 256 * j:T_seq + 256 * j + 128]),
                       reads=[d_qsp], writes=[qj], dma=True)
                P.emit("sync", dma(qoh[:, 128 * i:128 * i + 128],
                                   a_qsp[128 * i:128 * i + 128, 256 * j + 128:256 * j + 256]),
                       reads=[d_qsp], writes=[qoh], dma=True)
                P.emit("sync", dma(qol[:, 128 * i:128 * i + 128],
                                   a_qsp[128 * i:128 * i + 128,
                                         T_seq + 256 * j + 128:T_seq + 256 * j + 256]),
                       reads=[d_qsp], writes=[qol], dma=True)
            P.emit("gpsimd", (lambda qj: (lambda e: e.tensor_scalar(
                qj[:, :], qj[:, :], selb[:, 0:1], None, ALU.mult)))(qj),
                   reads=[qj, selb], writes=[qj])
            P.emit("gpsimd", (lambda a: (lambda e: e.tensor_scalar(
                a[:, 0:1024], a[:, 0:1024], selb[:, 1:2], None, ALU.mult)))(qoh),
                   reads=[qoh, selb], writes=[qoh])
            P.emit("gpsimd", (lambda a: (lambda e: e.tensor_scalar(
                a[:, 0:1024], a[:, 0:1024], selb[:, 1:2], None, ALU.mult)))(qol),
                   reads=[qol, selb], writes=[qol])
            P.emit("gpsimd", (lambda qj, a: (lambda e: e.tensor_add(
                qj[:, 0:1024], qj[:, 0:1024], a[:, 0:1024])))(qj, qoh),
                   reads=[qj, qoh], writes=[qj])
            P.emit("gpsimd", (lambda qj, a: (lambda e: e.tensor_add(
                qj[:, 1024:2048], qj[:, 1024:2048], a[:, 0:1024])))(qj, qol),
                   reads=[qj, qol], writes=[qj])
            for g in range(ngrp):
                w = min(512, nk - 512 * g)
                kset = ksets[g % 2]
                for i in range(IC):
                    P.emit("sync", dma(kset[i][:, 0:w],
                                       a_ksp[128 * i:128 * i + 128, 512 * g:512 * g + w]),
                           reads=[d_ksp], writes=[kset[i]], dma=True)
                    P.emit("sync", dma(kset[i][:, 1024:1024 + w],
                                       a_ksp[128 * i:128 * i + 128,
                                             T_seq + 512 * g:T_seq + 512 * g + w]),
                           reads=[d_ksp], writes=[kset[i]], dma=True)
                sb = PB[g % 2]
                for i in range(IC):
                    qh_ = qj[:, 128 * i:128 * i + 128]
                    ql_ = qj[:, 1024 + 128 * i:1024 + 128 * i + 128]
                    P.emit("tensor", mm(sb[:, 0:w], qh_, kset[i][:, 0:w],
                                        i == 0, False),
                           reads=[qj, kset[i]], writes=[sb])
                    P.emit("tensor", mm(sb[:, 0:w], qh_, kset[i][:, 1024:1024 + w],
                                        False, False),
                           reads=[qj, kset[i]], writes=[sb])
                    P.emit("tensor", mm(sb[:, 0:w], ql_, kset[i][:, 0:w],
                                        False, i == IC - 1),
                           reads=[qj, kset[i]], writes=[sb])
                P.emit("scalar", (lambda a, b: (lambda e: e.copy(a, b)))(
                    ys[g][:, 0:w], sb[:, 0:w]),
                       reads=[sb], writes=[ys[g]])
            for (tt, mk) in ((2 * j, maskA), (2 * j + 1, maskB)):
                g4, r4 = tt // 4, tt % 4
                P.emit("vector", (lambda a, m: (lambda e: e.tensor_add(a, a, m)))(
                    ys[g4][:, 128 * r4:128 * r4 + 128], mk[:, :]),
                       reads=[ys[g4], mk], writes=[ys[g4]])
            for g in range(ngrp):
                w = min(512, nk - 512 * g)
                P.emit("vector", (lambda g, w: (lambda e: e.reduce_max(
                    rm_sb[:, g:g + 1], ys[g][:, 0:w], axis=AX.X)))(g, w),
                       reads=[ys[g]], writes=[rm_sb])
            P.emit("vector", (lambda ngrp: (lambda e: e.reduce_max(
                nm_sb[:, 0:1], rm_sb[:, 0:ngrp], axis=AX.X, negate=True)))(ngrp),
                   reads=[rm_sb], writes=[nm_sb])
            for g in range(ngrp):
                w = min(512, nk - 512 * g)
                P.emit("scalar", (lambda g, w: (lambda e: e.activation(
                    ys[g][:, 0:w], ys[g][:, 0:w], AF.Exp, bias=nm_sb[:, 0:1],
                    accum_out=sm_sb[:, g:g + 1])))(g, w),
                       reads=[ys[g], nm_sb], writes=[ys[g], sm_sb])
            P.emit("vector", (lambda ngrp: (lambda e: e.reduce_sum(
                ls_sb[:, 0:1], sm_sb[:, 0:ngrp], axis=AX.X)))(ngrp),
                   reads=[sm_sb], writes=[ls_sb])
            P.emit("vector", lambda e: e.reciprocal(rl_sb[:, 0:1], ls_sb[:, 0:1]),
                   reads=[ls_sb], writes=[rl_sb])
            for g in range(ngrp):
                w = min(512, nk - 512 * g)
                P.emit("vector", (lambda g, w: (lambda e: e.tensor_scalar(
                    ys[g][:, 0:w], ys[g][:, 0:w], rl_sb[:, 0:1], None, ALU.mult)))(g, w),
                       reads=[ys[g], rl_sb], writes=[ys[g]])

        def pv_part(j):
            nkt = 2 * j + 2
            for kt in range(nkt):
                g4, r4 = kt // 4, kt % 4
                P.emit("sync", dma(vbuf[kt % 2][:, :],
                                   a_vsp[128 * kt:128 * kt + 128, :]),
                       reads=[d_vsp], writes=[vbuf[kt % 2]], dma=True)
                tb = PB[4 + kt % 2]
                P.emit("tensor", (lambda tb, g4, r4: (lambda e: e.transpose(
                    tb[:, 0:128], ys[g4][:, 128 * r4:128 * r4 + 128], ident[:, :])))(tb, g4, r4),
                       reads=[ys[g4], ident], writes=[tb])
                P.emit("scalar", (lambda a, tb: (lambda e: e.copy(a, tb[:, 0:128])))(
                    pth[kt % 2][:, :], tb),
                       reads=[tb], writes=[pth[kt % 2]])
                P.emit("vector", (lambda a, tb, b: (lambda e: e.tensor_sub(a, tb[:, 0:128], b)))(
                    ptl[kt % 2][:, :], tb, pth[kt % 2][:, :]),
                       reads=[tb, pth[kt % 2]], writes=[ptl[kt % 2]])
                for cc in range(8):
                    ob = PB[2 + cc // 4]
                    osl = slice(128 * (cc % 4), 128 * (cc % 4) + 128)
                    vh_ = vbuf[kt % 2][:, 128 * cc:128 * cc + 128]
                    vl_ = vbuf[kt % 2][:, 1024 + 128 * cc:1024 + 128 * cc + 128]
                    P.emit("tensor", mm(ob[:, osl], vh_, pth[kt % 2][:, :],
                                        start=(kt == 0 and cc % 4 == 0), stop=False),
                           reads=[vbuf[kt % 2], pth[kt % 2]], writes=[ob])
                    P.emit("tensor", mm(ob[:, osl], vh_, ptl[kt % 2][:, :],
                                        start=False, stop=False),
                           reads=[vbuf[kt % 2], ptl[kt % 2]], writes=[ob])
                    P.emit("tensor", mm(ob[:, osl], vl_, pth[kt % 2][:, :],
                                        start=False,
                                        stop=(kt == nkt - 1 and cc % 4 == 3)),
                           reads=[vbuf[kt % 2], pth[kt % 2]], writes=[ob])
            for cc in range(8):
                ob = PB[2 + cc // 4]
                osl = slice(128 * (cc % 4), 128 * (cc % 4) + 128)
                dst = oslot[cc // 2]
                base = 1024 * (cc % 2)
                dh = slice(base + 128 * (j % 4), base + 128 * (j % 4) + 128)
                dl = slice(base + 512 + 128 * (j % 4), base + 512 + 128 * (j % 4) + 128)
                P.emit("scalar", (lambda a, b: (lambda e: e.copy(a, b)))(
                    dst[:, dh], ob[:, osl]),
                       reads=[ob], writes=[dst])
                P.emit("vector", (lambda a, b, c: (lambda e: e.tensor_sub(a, b, c)))(
                    dst[:, dl], ob[:, osl], dst[:, dh]),
                       reads=[ob, dst], writes=[dst])

        for j in range(NQ):
            s_part(j)
            pv_part(j)
            if j % 4 == 3:
                g2 = j // 4
                xm3 = a_xmsp.rearrange("p (a c) -> p a c", c=128)
                for oc in range(IC):
                    P.emit("sync", dma(xs[oc][:, 0:512].rearrange("p (a c) -> p a c", c=128),
                                       xm3[128 * oc:128 * oc + 128, 8 * g2:8 * g2 + 8:2, :]),
                           reads=[d_xmsp], writes=[xs[oc]], dma=True)
                    P.emit("sync", dma(ys[oc][:, 0:512].rearrange("p (a c) -> p a c", c=128),
                                       xm3[128 * oc:128 * oc + 128, 8 * g2 + 1:8 * g2 + 8:2, :]),
                           reads=[d_xmsp], writes=[ys[oc]], dma=True)
                for oc in range(IC):
                    P.emit("gpsimd", (lambda a: (lambda e: e.tensor_scalar(
                        a, a, selb[:, 0:1], None, ALU.mult)))(xs[oc][:, 0:512]),
                           reads=[xs[oc], selb], writes=[xs[oc]])
                    P.emit("gpsimd", (lambda a: (lambda e: e.tensor_scalar(
                        a, a, selb[:, 1:2], None, ALU.mult)))(ys[oc][:, 0:512]),
                           reads=[ys[oc], selb], writes=[ys[oc]])
                    P.emit("gpsimd", (lambda a, b: (lambda e: e.tensor_add(a, a, b)))(
                        xs[oc][:, 0:512], ys[oc][:, 0:512]),
                           reads=[xs[oc], ys[oc]], writes=[xs[oc]])
                for oc in range(IC):
                    pb = PB[6 + oc % 2]
                    for i in range(IC):
                        oh_ = oslot[i // 2][:, 1024 * (i % 2):1024 * (i % 2) + 512]
                        ol_ = oslot[i // 2][:, 1024 * (i % 2) + 512:1024 * (i % 2) + 1024]
                        wph_ = wp[i][:, 128 * oc:128 * oc + 128]
                        wpl_ = wp[i][:, 1024 + 128 * oc:1024 + 128 * oc + 128]
                        P.emit("tensor", mm(pb[:, 0:512], wph_, oh_, i == 0, False),
                               reads=[wp[i], oslot[i // 2]], writes=[pb])
                        P.emit("tensor", mm(pb[:, 0:512], wph_, ol_, False, False),
                               reads=[wp[i], oslot[i // 2]], writes=[pb])
                        P.emit("tensor", mm(pb[:, 0:512], wpl_, oh_, False, i == IC - 1),
                               reads=[wp[i], oslot[i // 2]], writes=[pb])
                    P.emit("vector", (lambda a, pb, oc: (lambda e: e.scalar_tensor_tensor(
                        a, pb[:, 0:512], bp_sb[:, oc:oc + 1], a, ALU.add, ALU.add)))(
                            xs[oc][:, 0:512], pb, oc),
                           reads=[pb, bp_sb, xs[oc]], writes=[xs[oc]])
                    P.emit("scalar", dma(a_x2sp[128 * oc:128 * oc + 128, 512 * g2:512 * g2 + 512],
                                         xs[oc][:, 0:512]),
                           reads=[xs[oc]], writes=[d_x2sp], dma=True)

        for st in range(NST3 if do_p3 else 0):
            csl = slice(TS * st, TS * st + TS)
            for i in range(IC):
                P.emit("sync", dma(xs[i][:, 0:TS], a_x2sp[128 * i:128 * i + 128, csl]),
                       reads=[d_x2sp], writes=[xs[i]], dma=True)
            emit_ln(xs, xlh, xll)
            emit_router(xlh, xll)
            emit_ffn(xlh, xll, ys)
            for oc in range(IC):
                P.emit("vector", (lambda a, b: (lambda e: e.scalar_tensor_tensor(
                    a, b, 0.5, a, ALU.mult, ALU.add)))(xs[oc][:, 0:TS], ys[oc][:, 0:TS]),
                       reads=[ys[oc], xs[oc]], writes=[xs[oc]])
            for oc in range(IC):
                P.emit("scalar", dma(a_out[128 * oc:128 * oc + 128, csl], xs[oc][:, 0:TS]),
                       reads=[xs[oc]], writes=[d_out], dma=True)

        import os as _os2
        _trunc = _os2.environ.get("TRUNC")
        if _trunc is None:
            for dest in list(P.dma_sems):
                P.barrier_dma("scalar", dest)
        with nc.Block() as block:
            P.play(block, {e: sem_auto[e] for e in ENGS},
                   trunc=None if _trunc is None else int(_trunc))

    n_inst = sum(len(v) for v in P.q.values())
    build.last_prog = P
    return nc, n_inst


def _prep_inputs(L0_attn_x, L0_attn_w_attn, L0_attn_b_attn, L0_attn_w_proj,
                 L0_attn_b_proj, L0_ffn_router_w, L0_ffn_router_b, L0_ffn_w1,
                 L0_ffn_b1, L0_ffn_w2, L0_ffn_b2, n_cores=8):
    f = np.float32
    x = np.asarray(L0_attn_x, f)
    shared = {
        "w_attn": np.ascontiguousarray(np.asarray(L0_attn_w_attn, f)),
        "ba_r": np.ascontiguousarray(np.asarray(L0_attn_b_attn, f).reshape(24, 128).T),
        "bav_row": np.ascontiguousarray(np.asarray(L0_attn_b_attn, f)[2048:].reshape(1, 1024)),
        "w_proj": np.ascontiguousarray(np.asarray(L0_attn_w_proj, f)),
        "bp_r": np.ascontiguousarray(np.asarray(L0_attn_b_proj, f).reshape(8, 128).T),
        "rw_t": np.ascontiguousarray(np.asarray(L0_ffn_router_w, f).T),
        "rb_r": np.ascontiguousarray(np.asarray(L0_ffn_router_b, f).reshape(16, 1)),
        "w1_t": np.ascontiguousarray(np.asarray(L0_ffn_w1, f).T),
        "b1_r": np.ascontiguousarray(np.asarray(L0_ffn_b1, f).reshape(32, 128).T),
        "w2_t": np.ascontiguousarray(np.asarray(L0_ffn_w2, f).T),
        "b2_r": np.ascontiguousarray(np.asarray(L0_ffn_b2, f).reshape(8, 128).T),
    }
    in_maps = []
    for c in range(n_cores):
        b, p = c // 2, c % 2
        m = dict(shared)
        m["x_t"] = np.ascontiguousarray(x[b].T)
        m["sel"] = np.array([[1.0 - p, float(p)]], f)
        in_maps.append(m)
    return in_maps


def _assemble(results, T_seq, n_cores=8):
    B = n_cores // 2
    out = np.empty((B, T_seq, C), np.float32)
    nb = T_seq // 128
    for b in range(B):
        full_t = np.empty((C, nb, 128), np.float32)
        for p in range(2):
            o = results[2 * b + p]["out_t"].reshape(C, nb // 2, 128)
            full_t[:, p::2, :] = o
        out[b] = full_t.reshape(C, T_seq).T
    return out


def kernel(**inputs):
    in_maps = _prep_inputs(**inputs)
    nc, _ = build(4096)
    res = run_bass_kernel_spmd(nc, in_maps, core_ids=list(range(8)))
    return _assemble(res.results, 4096)
